# version 06
# baseline (speedup 1.0000x reference)
import numpy as np
import concourse.bass as bass
import concourse.mybir as mybir
from concourse.bass_utils import run_bass_kernel_spmd

F32 = mybir.dt.float32
AF = mybir.ActivationFunctionType
ALU = mybir.AluOpType
AX = mybir.AxisListType

B, S, D = 4, 8192, 1024
H, K, M = 16, 64, 256
FF = 4096
NE = 4
DC = D // 128
FC = FF // 128
TILE = 512
GRP = 128
NG = TILE // GRP
T_OWN = S // 2
NT_A = S // TILE
NT_B = T_OWN // TILE
EPS0 = 1e-5
EPS1 = 1e-6
SCALE = float(np.float32(float(K) ** (-0.25)))
LN16 = float(np.log(16.0))

ENGINES = ("tensor", "scalar", "vector", "gpsimd", "sync")
DMA_SEMS = ("wc", "wa0", "wa1", "wa2", "wa3", "wb0", "wb1", "xd", "od")


class Plan:

    def __init__(self, nc):
        self.nc = nc
        self.sems = {e: nc.alloc_semaphore(name=f"sem_{e}") for e in ENGINES}
        for s in DMA_SEMS:
            self.sems[s] = nc.alloc_semaphore(name=f"sem_{s}")
        self.cnt = {k: 0 for k in self.sems}
        self.ops = {e: [] for e in ENGINES}
        self.writer = {}
        self.reader = {}

    def step(self, eng, fn, reads=(), writes=(), deps=(), sem=None):
        sem = sem or eng
        amt = 16 if sem in DMA_SEMS else 1
        dep = {}
        def add(ev):
            if ev is None:
                return
            s, v = ev
            if dep.get(s, -1) < v:
                dep[s] = v
        for ev in deps:
            add(ev)
        for r in reads:
            for ev in self.writer.get(r, {}).items():
                add(ev)
        for w in writes:
            for ev in self.writer.get(w, {}).items():
                add(ev)
            for ev in self.reader.get(w, {}).items():
                add(ev)
        if eng != "gpsimd" and eng in dep and self.cnt[eng] - dep[eng] >= 3:
            del dep[eng]
        self.cnt[sem] += amt
        ev = (sem, self.cnt[sem])
        self.ops[eng].append((fn, sorted(dep.items()), sem, ev[1]))
        for r in reads:
            self.reader.setdefault(r, {})[sem] = ev[1]
        for w in writes:
            self.writer[w] = {sem: ev[1]}
            self.reader[w] = {}
        return ev

    def alias(self, newkey, oldkeys):
        w = dict(self.writer.get(newkey, {}))
        r = dict(self.reader.get(newkey, {}))
        for ok in oldkeys:
            for s2, v in self.writer.get(ok, {}).items():
                w[s2] = max(w.get(s2, -1), v)
            for s2, v in self.reader.get(ok, {}).items():
                r[s2] = max(r.get(s2, -1), v)
        self.writer[newkey] = w
        self.reader[newkey] = r

    def dma(self, eng, sem, dst, src, reads=(), writes=(), deps=()):
        prev = getattr(self, "_last_dma", {}).get(sem)
        if not hasattr(self, "_last_dma"):
            self._last_dma = {}
        ev = self.step(eng, lambda e, dst=dst, src=src: e.dma_start(out=dst, in_=src),
                       reads=reads, writes=writes,
                       deps=list(deps) + ([prev] if prev else []), sem=sem)
        self._last_dma[sem] = ev
        return ev

    def build(self):
        nc = self.nc
        sems = self.sems
        with nc.Block() as block:
            for eng in ENGINES:
                oplist = self.ops[eng]
                def make(oplist=oplist, eng=eng):
                    def emit(e):
                        waited = {}
                        for fn, deps, sem, val in oplist:
                            for dsem, dval in deps:
                                if waited.get(dsem, -1) >= dval:
                                    continue
                                e.wait_ge(sems[dsem], dval)
                                waited[dsem] = dval
                            instr = fn(e)
                            instr.then_inc(sems[sem], 16 if sem in DMA_SEMS else 1)
                    return emit
                getattr(block, eng)(make())


def build_core_program(debug=False):
    nc = bass.Bass("TRN2", target_bir_lowering=False, detect_race_conditions=False)
    p = Plan(nc)

    dT = lambda n, s, k="ExternalInput", dt=F32: nc.dram_tensor(n, s, dt, kind=k).ap()
    xTo = dT("xTo", [D, T_OWN])
    wq = dT("wq", [D, D]); wk = dT("wk", [D, D]); wv = dT("wv", [D, D])
    wo = dT("wo", [D, D])
    omegaT = dT("omegaT", [K, M])
    ln1_g = dT("ln1_g", [128, DC]); ln1_b = dT("ln1_b", [128, DC])
    ln2_g = dT("ln2_g", [128, DC]); ln2_b = dT("ln2_b", [128, DC])
    w1d = dT("w1", [D, FF]); b1d = dT("b1", [128, FC])
    w2d = dT("w2", [FF, D]); b2d = dT("b2", [128, DC])
    wrd = dT("wr", [D, NE]); brd = dT("br", [NE, 1])
    we1d = dT("we1", [NE, D, FF]); be1d = dT("be1", [128, NE, FC])
    we2d = dT("we2", [NE, FF, D]); be2d = dT("be2", [128, NE, DC])
    yT = dT("yT", [D, T_OWN], k="ExternalOutput")
    kvb_in = nc.dram_tensor("kvb_in", [K + 1, 8 * TILE], F32, kind="Internal").ap()
    kvb_out = nc.dram_tensor("kvb_out", [K + 1, 8 * TILE], F32, kind="Internal").ap()

    sb = lambda n, s, dt=F32: nc.alloc_sbuf_tensor(n, s, dt)
    xn_sb = sb("xn_sb", [128, DC, TILE])
    u_sb = sb("u_sb", [128, DC, TILE])
    t1_sb = sb("t1_sb", [128, DC, TILE])
    cat_sb = sb("cat_sb", [128, DC, TILE])
    h1_sb = sb("h1_sb", [128, FC, TILE])
    x2scr = sb("x2scr", [128, 2, TILE])
    sq_sb = sb("sq_sb", [128, TILE])
    sqT_sb = sb("sqT_sb", [128, NG, 16])
    stab_sb = sb("stab_sb", [128, 2, 2])
    bias_sb = sb("bias_sb", [128, 2, 2])
    pk_sb = sb("pk_sb", [128, 2, TILE])
    den_sb = sb("den_sb", [128, 2, 2])
    vzbuf = sb("vzbuf", [128, NG * H * (K + 1)])
    kvzbuf = sb("kvzbuf", [128, 8 * TILE])
    kvM_sb = sb("kvM_sb", [128, H, 2, K])
    zbcb_sb = sb("zbcb_sb", [128, 8, M])
    stats = sb("stats", [1, 2, TILE])
    lg_sb = sb("lg_sb", [128, 2 * NE])
    mx_sb = sb("mx_sb", [128, 1])
    msk_sb = sb("msk_sb", [128, NE + 1])
    wA_sb = sb("wA_sb", [128, 2, DC, 128])
    wB_sb = sb("wB_sb", [128, 2, 4, 128])
    wr_sb = sb("wr_sb", [128, DC, NE])
    ob_sb = sb("ob_sb", [128, 2 * M])
    ident = sb("ident", [128, 128])
    ones1 = sb("ones1", [128, 1])
    onesr = sb("onesr", [1, 128])
    blk_sb = sb("blk_sb", [128, DC, 16])
    gb_sb = sb("gb_sb", [128, 4 * DC])
    b1_sb = sb("b1_sb", [128, FC])
    b2_sb = sb("b2_sb", [128, DC])
    br_sb = sb("br_sb", [NE, 1])
    be1_sb = sb("be1_sb", [128, NE, FC])
    be2_sb = sb("be2_sb", [128, NE, DC])
    epsc_sb = sb("epsc_sb", [1, 2])

    vz_sb = vzbuf.ap().rearrange("p (g h k) -> p g h k", g=NG, h=H)
    pqT_sb = vzbuf.ap()[:, 0:4 * TILE].rearrange("p (c t) -> p c t", c=4)
    zbca_sb = vzbuf.ap()[:, 4 * TILE:4 * TILE + 8 * M].rearrange(
        "p (h m) -> p h m", h=8)
    zbc_ap = lambda h: zbca_sb[:, h] if h < 8 else zbcb_sb[:, h - 8]
    kvz_sb = kvzbuf.ap().rearrange("p (r t) -> p r t", r=8)
    mbcE_sb = kvzbuf.ap()[:, 0:TILE]
    mscr_sb = kvzbuf.ap()[:, TILE:2 * TILE]
    lgf_sb = kvzbuf.ap()[0:NE, 2 * TILE:3 * TILE]
    zrow_sb = x2scr.ap()[0:1, 0, M:2 * M]
    pq_ap = lambda par: kvzbuf.ap()[:, 3 * TILE + par * M:3 * TILE + (par + 1) * M]

    pb = [nc.alloc_psum_tensor(f"pb{i}", [128, TILE], F32) for i in range(8)]

    class Pool:
        def __init__(self):
            self.order = list(range(8))
        def take(self):
            i = self.order.pop(0)
            return i, pb[i]
        def give(self, i):
            self.order.append(i)
    pool = Pool()

    p.step("gpsimd", lambda g: g.memset(ident[:], 0.0), writes=[("ident0",)])
    p.step("gpsimd", lambda g: g.affine_select(
        out=ident[:], in_=ident[:], compare_op=ALU.not_equal,
        fill=1.0, base=0, pattern=[[-1, 128]], channel_multiplier=1),
        reads=[("ident0",)], writes=[("ident",)])
    p.step("gpsimd", lambda g: g.memset(blk_sb[:], 0.0), writes=[("blk0",)])
    def blk_fill(g):
        for c in range(DC):
            g.memset(blk_sb[0:64, c, 2 * c:2 * c + 1], 0.5)
            instr = g.memset(blk_sb[64:128, c, 2 * c + 1:2 * c + 2], 0.5)
        return instr
    p.step("gpsimd", blk_fill, reads=[("blk0",)], writes=[("blk",)])
    def misc_consts(g):
        g.memset(ones1[:], 1.0)
        g.memset(onesr[:], 1.0)
        g.memset(ob_sb[:], 0.0)
        g.memset(epsc_sb[0:1, 0:1], EPS0)
        g.memset(epsc_sb[0:1, 1:2], EPS1)
        g.memset(sq_sb[:], 0.0)
        g.memset(kvzbuf[:], 0.0)
        for h in range(H):
            for gi in range(NG):
                instr = g.memset(vz_sb[:, gi, h, K:K + 1], 1.0)
        return instr
    p.step("gpsimd", misc_consts,
           writes=[("ones1",), ("onesr",), ("ob",), ("sq",), ("kvz",),
                   ("vzco",), ("epsc",)])

    p.dma("sync", "wc", ob_sb[0:64, 0:M], omegaT, writes=[("ob",)])
    p.dma("sync", "wc", ob_sb[64:128, M:2 * M], omegaT, writes=[("ob",)])
    p.dma("sync", "wc", gb_sb[:, 0:DC], ln1_g, writes=[("gb0",)])
    p.dma("sync", "wc", gb_sb[:, DC:2 * DC], ln1_b, writes=[("gb1",)])
    p.dma("sync", "wc", gb_sb[:, 2 * DC:3 * DC], ln2_g, writes=[("gb2",)])
    p.dma("sync", "wc", gb_sb[:, 3 * DC:4 * DC], ln2_b, writes=[("gb3",)])
    p.dma("sync", "wc", b1_sb[:], b1d, writes=[("b1",)])
    p.dma("sync", "wc", b2_sb[:], b2d, writes=[("b2",)])
    p.dma("sync", "wc", br_sb[:], brd, writes=[("br",)])
    p.dma("sync", "wc", be1_sb[:], be1d, writes=[("be1",)])
    p.dma("sync", "wc", be2_sb[:], be2d, writes=[("be2",)])
    p.dma("sync", "wc", wr_sb[:], wrd.rearrange("(kc p) e -> p kc e", p=128),
          writes=[("wr",)])

    LN1G, LN1B = gb_sb[:, 0:DC], gb_sb[:, DC:2 * DC]
    LN2G, LN2B = gb_sb[:, 2 * DC:3 * DC], gb_sb[:, 3 * DC:4 * DC]

    wslab = lambda w, oc: w.rearrange("(kc p) (oc f) -> p oc kc f", p=128, f=128)[:, oc]

    def layernorm_inc(src, dst, eps, skey, dkey, affine=None, affine_keys=()):
        ib1, bank1 = pool.take()
        ib2, bank2 = pool.take()

        def feed(c):
            ci = c % 2
            p.step("scalar", lambda s, c=c, ci=ci: s.activation(
                out=x2scr[:, ci], in_=src[:, c], func=AF.Square),
                reads=[skey], writes=[("x2", ci)])
            p.step("tensor", lambda t, c=c, b=bank1, st=(c == 0), sp=(c == DC - 1):
                   t.matmul(b[0:1, :], ones1[:], src[:, c], start=st, stop=sp),
                   reads=[skey, ("ones1",)], writes=[("pb", ib1)])
            p.step("tensor", lambda t, c=c, ci=ci, b=bank2, st=(c == 0), sp=(c == DC - 1):
                   t.matmul(b[0:1, :], ones1[:], x2scr[:, ci], start=st, stop=sp),
                   reads=[("x2", ci), ("ones1",)], writes=[("pb", ib2)])

        def finish():
            invD = 1.0 / D
            r1, r2 = bank1[0:1, :], bank2[0:1, :]
            p.step("vector", lambda v: v.tensor_scalar(
                out=stats[:, 0], in0=r1, scalar1=invD, scalar2=None, op0=ALU.mult),
                reads=[("pb", ib1)], writes=[("st0",)])
            p.step("scalar", lambda s: s.activation(
                out=stats[:, 1], in_=stats[:, 0], func=AF.Square),
                reads=[("st0",)], writes=[("st1",)])
            p.step("vector", lambda v: v.scalar_tensor_tensor(
                out=r2, in0=r2, scalar=invD, in1=stats[:, 1],
                op0=ALU.mult, op1=ALU.subtract),
                reads=[("st1",)], writes=[("pb", ib2)])
            eps_ap = epsc_sb[0:1, (0 if eps == EPS0 else 1):(1 if eps == EPS0 else 2)]
            p.step("scalar", lambda s, eps_ap=eps_ap: s.activation(
                out=r2, in_=r2, func=AF.Sqrt, bias=eps_ap),
                reads=[("epsc",)], writes=[("pb", ib2)])
            p.step("vector", lambda v: v.reciprocal(out=r2, in_=r2),
                   reads=[], writes=[("pb", ib2)])
            p.step("vector", lambda v: v.tensor_copy(stats[:, 1], r2),
                   reads=[("pb", ib2)], writes=[("st1",)])
            p.step("vector", lambda v: v.scalar_tensor_tensor(
                out=stats[:, 0], in0=stats[:, 0], scalar=-1.0, in1=stats[:, 1],
                op0=ALU.mult, op1=ALU.mult),
                reads=[("st1",)], writes=[("st0",)])
            pool.give(ib1); pool.give(ib2)
            ib3, bank3 = pool.take()
            ib4, bank4 = pool.take()
            p.step("tensor", lambda t, b=bank3: t.matmul(
                b[:], onesr[:], stats[:, 1], start=True, stop=True),
                reads=[("st1",), ("onesr",)], writes=[("pb", ib3)])
            p.step("tensor", lambda t, b=bank4: t.matmul(
                b[:], onesr[:], stats[:, 0], start=True, stop=True),
                reads=[("st0",), ("onesr",)], writes=[("pb", ib4)])
            for c in range(DC):
                p.step("vector", lambda v, c=c, b=bank3: v.tensor_tensor(
                    out=dst[:, c], in0=src[:, c], in1=b[:], op=ALU.mult),
                    reads=[skey, ("pb", ib3)], writes=[dkey])
                p.step("vector", lambda v, c=c, b=bank4: v.tensor_tensor(
                    out=dst[:, c], in0=dst[:, c], in1=b[:], op=ALU.add),
                    reads=[("pb", ib4)], writes=[dkey])
                if affine is not None:
                    g_ap, b_ap = affine
                    p.step("vector", lambda v, c=c: v.tensor_scalar(
                        out=dst[:, c], in0=dst[:, c], scalar1=g_ap[:, c:c + 1],
                        scalar2=b_ap[:, c:c + 1], op0=ALU.mult, op1=ALU.add),
                        reads=list(affine_keys), writes=[dkey])
            pool.give(ib3); pool.give(ib4)

        return feed, finish

    def layernorm(src, dst, eps, skey, dkey, affine=None, affine_keys=()):
        ib1, bank1 = pool.take()
        ib2, bank2 = pool.take()
        for c in range(DC):
            ci = c % 2
            p.step("scalar", lambda s, c=c, ci=ci: s.activation(
                out=x2scr[:, ci], in_=src[:, c], func=AF.Square),
                reads=[skey], writes=[("x2", ci)])
            p.step("tensor", lambda t, c=c, b=bank1, st=(c == 0), sp=(c == DC - 1):
                   t.matmul(b[0:1, :], ones1[:], src[:, c], start=st, stop=sp),
                   reads=[skey, ("ones1",)], writes=[("pb", ib1)])
            p.step("tensor", lambda t, c=c, ci=ci, b=bank2, st=(c == 0), sp=(c == DC - 1):
                   t.matmul(b[0:1, :], ones1[:], x2scr[:, ci], start=st, stop=sp),
                   reads=[("x2", ci), ("ones1",)], writes=[("pb", ib2)])
        invD = 1.0 / D
        r1, r2 = bank1[0:1, :], bank2[0:1, :]
        p.step("vector", lambda v: v.tensor_scalar(
            out=stats[:, 0], in0=r1, scalar1=invD, scalar2=None, op0=ALU.mult),
            reads=[("pb", ib1)], writes=[("st0",)])
        p.step("scalar", lambda s: s.activation(
            out=stats[:, 1], in_=stats[:, 0], func=AF.Square),
            reads=[("st0",)], writes=[("st1",)])
        p.step("vector", lambda v: v.scalar_tensor_tensor(
            out=r2, in0=r2, scalar=invD, in1=stats[:, 1],
            op0=ALU.mult, op1=ALU.subtract),
            reads=[("st1",)], writes=[("pb", ib2)])
        eps_ap = epsc_sb[0:1, (0 if eps == EPS0 else 1):(1 if eps == EPS0 else 2)]
        p.step("scalar", lambda s, eps_ap=eps_ap: s.activation(
            out=r2, in_=r2, func=AF.Sqrt, bias=eps_ap),
            reads=[("epsc",)], writes=[("pb", ib2)])
        p.step("vector", lambda v: v.reciprocal(out=r2, in_=r2),
               reads=[], writes=[("pb", ib2)])
        p.step("vector", lambda v: v.tensor_copy(stats[:, 1], r2),
               reads=[("pb", ib2)], writes=[("st1",)])
        p.step("vector", lambda v: v.scalar_tensor_tensor(
            out=stats[:, 0], in0=stats[:, 0], scalar=-1.0, in1=stats[:, 1],
            op0=ALU.mult, op1=ALU.mult),
            reads=[("st1",)], writes=[("st0",)])
        pool.give(ib1); pool.give(ib2)
        ib3, bank3 = pool.take()
        ib4, bank4 = pool.take()
        p.step("tensor", lambda t, b=bank3: t.matmul(
            b[:], onesr[:], stats[:, 1], start=True, stop=True),
            reads=[("st1",), ("onesr",)], writes=[("pb", ib3)])
        p.step("tensor", lambda t, b=bank4: t.matmul(
            b[:], onesr[:], stats[:, 0], start=True, stop=True),
            reads=[("st0",), ("onesr",)], writes=[("pb", ib4)])
        for c in range(DC):
            p.step("vector", lambda v, c=c, b=bank3: v.tensor_tensor(
                out=dst[:, c], in0=src[:, c], in1=b[:], op=ALU.mult),
                reads=[skey, ("pb", ib3)], writes=[dkey])
            p.step("vector", lambda v, c=c, b=bank4: v.tensor_tensor(
                out=dst[:, c], in0=dst[:, c], in1=b[:], op=ALU.add),
                reads=[("pb", ib4)], writes=[dkey])
            if affine is not None:
                g_ap, b_ap = affine
                p.step("vector", lambda v, c=c: v.tensor_scalar(
                    out=dst[:, c], in0=dst[:, c], scalar1=g_ap[:, c:c + 1],
                    scalar2=b_ap[:, c:c + 1], op0=ALU.mult, op1=ALU.add),
                    reads=list(affine_keys), writes=[dkey])
        pool.give(ib3); pool.give(ib4)

    def qk_proj(w_dram, skey):
        sqi, sqb = pool.take()
        for par in range(2):
            p.dma("sync", f"wa{par}", wA_sb[:, par], wslab(w_dram, par),
                  writes=[("wA", par)])
        for oc in range(DC):
            par = oc % 2
            ib, bank = pool.take()
            for kc in range(DC):
                p.step("tensor", lambda t, par=par, kc=kc, b=bank,
                       st=(kc == 0), sp=(kc == DC - 1):
                       t.matmul(b[:], wA_sb[:, par, kc], xn_sb[:, kc],
                                start=st, stop=sp),
                       reads=[("wA", par), skey], writes=[("pb", ib)])
            p.step("scalar", lambda s, oc=oc, b=bank: s.activation(
                out=u_sb[:, oc], in_=b[:], func=AF.Copy, scale=SCALE),
                reads=[("pb", ib)], writes=[("u",)])
            p.step("scalar", lambda s, oc=oc, b=bank, ci=oc % 2: s.activation(
                out=x2scr[:, ci], in_=b[:], func=AF.Square, scale=SCALE),
                reads=[("pb", ib)], writes=[("x2", oc % 2)])
            p.step("tensor", lambda t, oc=oc, ci=oc % 2, st=(oc == 0),
                   sp=(oc == DC - 1), b=sqb:
                   t.matmul(b[0:16, :], blk_sb[:, oc], x2scr[:, ci],
                            start=st, stop=sp),
                   reads=[("x2", oc % 2), ("blk",)], writes=[("pb", sqi)])
            pool.give(ib)
            if oc + 2 < DC:
                p.dma("sync", f"wa{par}", wA_sb[:, par], wslab(w_dram, oc + 2),
                      writes=[("wA", par)])
        p.step("vector", lambda v, b=sqb: v.tensor_copy(sq_sb[0:16, :], b[0:16, :]),
               reads=[("pb", sqi)], writes=[("sq",)])
        pool.give(sqi)
        for gi in range(NG):
            ib, bank = pool.take()
            p.step("tensor", lambda t, gi=gi, b=bank: t.transpose(
                b[:, 0:128], sq_sb[:, gi * GRP:(gi + 1) * GRP], ident[:]),
                reads=[("sq",), ("ident",)], writes=[("pb", ib)])
            p.step("vector", lambda v, gi=gi, b=bank: v.tensor_copy(
                sqT_sb[:, gi], b[:, 0:16]),
                reads=[("pb", ib)], writes=[("sqT",)])
            pool.give(ib)

    w4bufs = [wA_sb.ap()[:, 0, 0:4], wA_sb.ap()[:, 0, 4:8],
              wA_sb.ap()[:, 1, 0:4], wA_sb.ap()[:, 1, 4:8],
              wB_sb.ap()[:, 0], wB_sb.ap()[:, 1]]
    w4sems = ["wa0", "wa1", "wa2", "wa3", "wb0", "wb1"]

    def w2_alias_in():
        p.alias(("w4", 0), [("wA", 0)])
        p.alias(("w4", 1), [("wA", 0)])
        p.alias(("w4", 2), [("wA", 1)])
        p.alias(("w4", 3), [("wA", 1)])

    def w2_alias_out():
        p.alias(("wA", 0), [("w4", 0), ("w4", 1)])
        p.alias(("wA", 1), [("w4", 2), ("w4", 3)])

    def w2_big(wd, evac_cb, only_oc=None):
        w2_alias_in()
        ocs = range(DC) if only_oc is None else [only_oc]
        ret = None
        for oc in ocs:
            ib, bank = pool.take()
            for j in range(8):
                q = (oc * 8 + j) % 6
                p.dma("sync", w4sems[q], w4bufs[q],
                      wslab(wd, oc).rearrange("p (j kc) f -> p j kc f", j=8)[:, j],
                      writes=[("w4", q)])
                for kc4 in range(4):
                    kc = j * 4 + kc4
                    p.step("tensor", lambda t, q=q, kc4=kc4, kc=kc, b=bank,
                           st=(kc == 0), sp=(kc == FC - 1):
                           t.matmul(b[:], w4bufs[q][:, kc4], h1_sb[:, kc],
                                    start=st, stop=sp),
                           reads=[("w4", q), ("h1",)], writes=[("pb", ib)])
            if evac_cb is None:
                ret = (ib, bank)
            else:
                evac_cb(oc, bank, ib)
                pool.give(ib)
        w2_alias_out()
        return ret

    def phi_exp(pr, gi):
        gp = gi % 2
        ib, bank = pool.take()
        p.step("tensor", lambda t, pr=pr, gi=gi, b=bank: t.matmul(
            b[:], u_sb[:, pr, gi * GRP:(gi + 1) * GRP], ob_sb[:],
            start=True, stop=True),
            reads=[("u",), ("ob",)], writes=[("pb", ib)])
        p.step("vector", lambda v, b=bank, gp=gp: v.reduce_max(
            out=stab_sb[:, gp, 0:1], in_=b[:, 0:M], axis=AX.X),
            reads=[("pb", ib)], writes=[("stab", gp)])
        p.step("vector", lambda v, b=bank, gp=gp: v.reduce_max(
            out=stab_sb[:, gp, 1:2], in_=b[:, M:2 * M], axis=AX.X),
            reads=[("pb", ib)], writes=[("stab", gp)])
        p.step("vector", lambda v, pr=pr, gi=gi, gp=gp: v.tensor_tensor(
            out=bias_sb[:, gp], in0=sqT_sb[:, gi, 2 * pr:2 * pr + 2],
            in1=stab_sb[:, gp], op=ALU.add),
            reads=[("sqT",), ("stab", gp)], writes=[("bias", gp)])
        p.step("vector", lambda v, gp=gp: v.tensor_scalar(
            out=bias_sb[:, gp], in0=bias_sb[:, gp], scalar1=-1.0, scalar2=-LN16,
            op0=ALU.mult, op1=ALU.add),
            reads=[], writes=[("bias", gp)])
        for hh in range(2):
            p.step("scalar", lambda s, hh=hh, b=bank, gp=gp: s.activation(
                out=pk_sb[:, gp, hh * M:(hh + 1) * M],
                in_=b[:, hh * M:(hh + 1) * M],
                func=AF.Exp, bias=bias_sb[:, gp, hh:hh + 1]),
                reads=[("pb", ib), ("bias", gp)], writes=[("pk", gp)])
        pool.give(ib)

    xo_t = lambda t0: xTo.rearrange("(c p) s -> p c s", p=128)[:, :, t0:t0 + TILE]
    p.dma("scalar", "xd", cat_sb[:], xo_t(0), writes=[("cat",)])
    layernorm(cat_sb, xn_sb, EPS0, ("cat",), ("xn",))
    for ta in range(NT_B):
        t0 = ta * TILE
        qk_proj(wk, ("xn",))

        wAq = wA_sb.ap().rearrange("p a b c -> p (a b c)").rearrange(
            "p (kc f) -> p kc f", kc=DC)
        for q in range(4):
            p.dma("sync", "wa0",
                  wAq, wv.rearrange("(kc p) (q f) -> p q kc f", p=128, f=256)[:, q],
                  writes=[("wA", 0), ("wA", 1)])
            for gi in range(NG):
                ib, bank = pool.take()
                for kc in range(DC):
                    p.step("tensor", lambda t, kc=kc, gi=gi, b=bank,
                           st=(kc == 0), sp=(kc == DC - 1):
                           t.matmul(b[:, 0:256],
                                    xn_sb[:, kc, gi * GRP:(gi + 1) * GRP],
                                    wAq[:, kc], start=st, stop=sp),
                           reads=[("wA", 0), ("wA", 1), ("xn",)],
                           writes=[("pb", ib)])
                for j in range(4):
                    h = 4 * q + j
                    p.step("vector", lambda v, gi=gi, h=h, j=j, b=bank: v.tensor_copy(
                        vz_sb[:, gi, h, 0:K], b[:, j * K:(j + 1) * K]),
                        reads=[("pb", ib)], writes=[("vz", gi, h)])
                pool.give(ib)

        if ta + 1 < NT_B:
            p.dma("scalar", "xd", cat_sb[:], xo_t(t0 + TILE), writes=[("cat",)])
            layernorm(cat_sb, xn_sb, EPS0, ("cat",), ("xn",))

        for pr in range(8):
            kvi, kvb = pool.take()
            phi_exp(pr, 0)
            p.step("vector", lambda v: v.tensor_scalar(
                out=pk_sb[:, 0], in0=pk_sb[:, 0], scalar1=1e-6, scalar2=None,
                op0=ALU.add), reads=[], writes=[("pk", 0)])
            for gi in range(NG):
                gp = gi % 2
                if gi + 1 < NG:
                    phi_exp(pr, gi + 1)
                    p.step("vector", lambda v, gp2=(gi + 1) % 2: v.tensor_scalar(
                        out=pk_sb[:, gp2], in0=pk_sb[:, gp2], scalar1=1e-6,
                        scalar2=None, op0=ALU.add),
                        reads=[], writes=[("pk", (gi + 1) % 2)])
                for hh in range(2):
                    h = 2 * pr + hh
                    st = (gi == 0 and hh == 0)
                    sp = (gi == NG - 1 and hh == 1)
                    p.step("tensor", lambda t, h=h, hh=hh, gi=gi, st=st, sp=sp,
                           gp=gp, b=kvb: t.matmul(
                        b[0:K + 1, hh * M:(hh + 1) * M], vz_sb[:, gi, h],
                        pk_sb[:, gp, hh * M:(hh + 1) * M], start=st, stop=sp),
                        reads=[("pk", gp), ("vz", gi, h), ("vzco",)],
                        writes=[("pb", kvi)])
            p.step("vector", lambda v, pr=pr, b=kvb: v.tensor_tensor(
                out=kvz_sb[0:K + 1, pr], in0=kvz_sb[0:K + 1, pr],
                in1=b[0:K + 1, :], op=ALU.add),
                reads=[("pb", kvi)], writes=[("kvz",)])
            pool.give(kvi)

    p.dma("scalar", "xd", kvb_in, kvzbuf.ap()[0:K + 1, 0:8 * TILE],
          reads=[("kvz",)], writes=[("kvbi",)])
    p.step("gpsimd", lambda g: g.collective_compute(
        "AllReduce", ALU.add,
        replica_groups=[[0, 1], [2, 3], [4, 5], [6, 7]],
        ins=[kvb_in.opt()], outs=[kvb_out.opt()]),
        reads=[("kvbi",)], writes=[("kvbo",)])
    p.dma("scalar", "xd", kvzbuf.ap()[0:K + 1, 0:8 * TILE], kvb_out,
          reads=[("kvbo",)], writes=[("kvz",)])

    p.alias(("zrow",), [("x2", 0)])
    p.alias(("zbc",), [("vz", gi, h) for gi in range(NG) for h in range(H)]
            + [("vzco",)])
    for pr in range(8):
        for hh in range(2):
            h = 2 * pr + hh
            for mc in range(2):
                ib, bank = pool.take()
                p.step("tensor", lambda t, pr=pr, hh=hh, mc=mc, b=bank: t.transpose(
                    b[:, 0:128],
                    kvz_sb[:, pr, hh * M + mc * 128:hh * M + (mc + 1) * 128],
                    ident[:]),
                    reads=[("kvz",), ("ident",)], writes=[("pb", ib)])
                p.step("vector", lambda v, h=h, mc=mc, b=bank: v.tensor_copy(
                    kvM_sb[:, h, mc], b[:, 0:K]),
                    reads=[("pb", ib)], writes=[("kvM",)])
                p.step("vector", lambda v, b=bank: v.tensor_copy(
                    mx_sb[:, 0:1], b[:, K:K + 1]),
                    reads=[("pb", ib)], writes=[("zT",)])
                pool.give(ib)
                ib2, bank2 = pool.take()
                p.step("tensor", lambda t, b2=bank2: t.transpose(
                    b2[0:1, 0:128], mx_sb[:, 0:1], ident[:]),
                    reads=[("zT",), ("ident",)], writes=[("pb", ib2)])
                p.step("vector", lambda v, mc=mc, b2=bank2: v.tensor_copy(
                    zrow_sb[:, mc * 128:(mc + 1) * 128], b2[0:1, 0:128]),
                    reads=[("pb", ib2)], writes=[("zrow",)])
                pool.give(ib2)
            ib3, bank3 = pool.take()
            p.step("tensor", lambda t, b=bank3: t.matmul(
                b[:, 0:M], onesr[:], zrow_sb[:], start=True, stop=True),
                reads=[("zrow",), ("onesr",)], writes=[("pb", ib3)])
            p.step("vector", lambda v, h=h, b=bank3: v.tensor_copy(
                zbc_ap(h), b[:, 0:M]),
                reads=[("pb", ib3)], writes=[("zbc",)])
            pool.give(ib3)

    dbg_outs = []
    def dbg_dump(tag, ap_sb, rkey):
        if not debug:
            return
        dt_ = nc.dram_tensor(f"dbg_{tag}", list(ap_sb.shape), F32,
                             kind="ExternalOutput").ap()
        p.dma("scalar", "od", dt_, ap_sb, reads=[rkey])
        dbg_outs.append(tag)

    dbg_dump("kvM", kvM_sb.ap(), ("kvM",))
    dbg_dump("zbca", zbca_sb, ("zbc",))
    dbg_dump("zbcb", zbcb_sb.ap(), ("zbc",))

    p.alias(("mbcE",), [("kvz",)])
    p.alias(("mscr",), [("kvz",)])
    p.alias(("lgf",), [("kvz",)])
    for e in range(NE):
        p.alias(("mskrow", e), [("kvz",)])
    p.alias(("mx",), [("zT",)])
    p.alias(("pq", 0), [("kvz",)])
    p.alias(("pq", 1), [("kvz",)])
    p.alias(("x2", 0), [("zrow",)])
    vz_keys = [("vz", gi, h) for gi in range(NG) for h in range(H)] + \
        [("vzco",), ("zbc",)]
    for c in range(4):
        p.alias(("pqT", c), vz_keys)

    for tb in range(NT_B):
        t0 = tb * TILE
        p.dma("scalar", "xd", cat_sb[:], xo_t(t0), writes=[("cat",)])
        layernorm(cat_sb, xn_sb, EPS0, ("cat",), ("xn",))
        qk_proj(wq, ("xn",))

        def pq_chain(pr, gi):
            gp = gi % 2
            for hh in range(2):
                h = 2 * pr + hh
                p.step("vector", lambda v, hh=hh, gp=gp: v.tensor_scalar(
                    out=pq_ap(hh), in0=pk_sb[:, gp, hh * M:(hh + 1) * M],
                    scalar1=1e-6, scalar2=None, op0=ALU.add),
                    reads=[("pk", gp)], writes=[("pq", hh)])
                p.step("vector", lambda v, h=h, hh=hh: v.scalar_tensor_tensor(
                    out=x2scr[:, hh, 0:M], in0=pq_ap(hh), scalar=1.0,
                    in1=zbc_ap(h), op0=ALU.mult, op1=ALU.mult,
                    accum_out=den_sb[:, hh, 0:1]),
                    reads=[("pq", hh), ("zbc",)],
                    writes=[("x2", hh), ("den", hh)])
                p.step("vector", lambda v, hh=hh: v.reciprocal(
                    out=den_sb[:, hh, 1:2], in_=den_sb[:, hh, 0:1]),
                    reads=[], writes=[("den", hh)])
                p.step("vector", lambda v, hh=hh: v.tensor_scalar(
                    out=pq_ap(hh), in0=pq_ap(hh), scalar1=den_sb[:, hh, 1:2],
                    scalar2=None, op0=ALU.mult),
                    reads=[("den", hh)], writes=[("pq", hh)])
                for mc in range(2):
                    ib, bank = pool.take()
                    p.step("tensor", lambda t, mc=mc, hh=hh, b=bank: t.transpose(
                        b[:, 0:128], pq_ap(hh)[:, mc * 128:(mc + 1) * 128],
                        ident[:]),
                        reads=[("pq", hh), ("ident",)], writes=[("pb", ib)])
                    slot = 2 * hh + mc
                    p.step("vector", lambda v, slot=slot, gi=gi, b=bank:
                           v.tensor_copy(
                        pqT_sb[:, slot, gi * GRP:(gi + 1) * GRP], b[:, 0:128]),
                        reads=[("pb", ib)], writes=[("pqT", slot)])
                    pool.give(ib)

        for pr in range(8):
            phi_exp(pr, 0)
            for gi in range(NG):
                if gi + 1 < NG:
                    phi_exp(pr, gi + 1)
                pq_chain(pr, gi)
            ib, bank = pool.take()
            for hh in range(2):
                h = 2 * pr + hh
                for mc in range(2):
                    tp = (0, 64) if hh else None
                    p.step("tensor", lambda t, h=h, hh=hh, mc=mc, b=bank,
                           st=(mc == 0), sp=(mc == 1), tp=tp: t.matmul(
                        b[64 * hh:64 * hh + 64, :], kvM_sb[:, h, mc],
                        pqT_sb[:, 2 * hh + mc], start=st, stop=sp,
                        tile_position=tp),
                        reads=[("kvM",), ("pqT", 2 * hh + mc)],
                        writes=[("pb", ib)])
            p.step("scalar", lambda s, pr=pr, b=bank: s.activation(
                out=cat_sb[:, pr], in_=b[:], func=AF.Copy),
                reads=[("pb", ib)], writes=[("cat",)])
            pool.give(ib)

        if tb == 0:
            dbg_dump("xn", xn_sb.ap(), ("xn",))
            dbg_dump("uq", u_sb.ap(), ("u",))
            dbg_dump("sq", sq_sb.ap(), ("sq",))
            dbg_dump("cat", cat_sb.ap(), ("cat",))

        ln1_feed, ln1_fin = layernorm_inc(t1_sb, xn_sb, EPS1, ("t1",), ("xn",),
                                          affine=(LN1G, LN1B),
                                          affine_keys=[("gb0",), ("gb1",)])
        for par in range(2):
            p.dma("sync", f"wa{par}", wA_sb[:, par], wslab(wo, par),
                  writes=[("wA", par)])
        for oc in range(DC):
            par = oc % 2
            ib, bank = pool.take()
            for kc in range(DC):
                p.step("tensor", lambda t, par=par, kc=kc, b=bank,
                       st=(kc == 0), sp=(kc == DC - 1):
                       t.matmul(b[:], wA_sb[:, par, kc], cat_sb[:, kc],
                                start=st, stop=sp),
                       reads=[("wA", par), ("cat",)], writes=[("pb", ib)])
            p.step("vector", lambda v, oc=oc, b=bank: v.tensor_tensor(
                out=t1_sb[:, oc], in0=b[:], in1=xn_sb[:, oc], op=ALU.add),
                reads=[("pb", ib), ("xn",)], writes=[("t1",)])
            pool.give(ib)
            ln1_feed(oc)
            if oc + 2 < DC:
                p.dma("sync", f"wa{par}", wA_sb[:, par], wslab(wo, oc + 2),
                      writes=[("wA", par)])

        if tb == 0:
            dbg_dump("t1", t1_sb.ap(), ("t1",))

        p.dma("scalar", "xd", cat_sb[:],
              xTo.rearrange("(c p) s -> p c s", p=128)[:, :, t0:t0 + TILE],
              writes=[("cat",)])

        ln1_fin()

        if tb == 0:
            dbg_dump("x1", xn_sb.ap(), ("xn",))

        for par in range(2):
            p.dma("sync", f"wa{par}", wA_sb[:, par], wslab(w1d, par),
                  writes=[("wA", par)])
        for fc in range(FC):
            par = fc % 2
            ib, bank = pool.take()
            for kc in range(DC):
                p.step("tensor", lambda t, par=par, kc=kc, b=bank,
                       st=(kc == 0), sp=(kc == DC - 1):
                       t.matmul(b[:], wA_sb[:, par, kc], xn_sb[:, kc],
                                start=st, stop=sp),
                       reads=[("wA", par), ("xn",)], writes=[("pb", ib)])
            p.step("scalar", lambda s, fc=fc, b=bank: s.activation(
                out=h1_sb[:, fc], in_=b[:], func=AF.Gelu,
                bias=b1_sb[:, fc:fc + 1]),
                reads=[("pb", ib), ("b1",)], writes=[("h1",)])
            pool.give(ib)
            if fc + 2 < FC:
                p.dma("sync", f"wa{par}", wA_sb[:, par], wslab(w1d, fc + 2),
                      writes=[("wA", par)])

        if tb == 0:
            dbg_dump("h1", h1_sb.ap(), ("h1",))

        ln2_feed, ln2_fin = layernorm_inc(t1_sb, u_sb, EPS1, ("t1",), ("u",),
                                          affine=(LN2G, LN2B),
                                          affine_keys=[("gb2",), ("gb3",)])
        def w2_evac(oc, bank, ib):
            p.step("vector", lambda v, oc=oc, b=bank: v.scalar_tensor_tensor(
                out=t1_sb[:, oc], in0=b[:], scalar=b2_sb[:, oc:oc + 1],
                in1=xn_sb[:, oc], op0=ALU.add, op1=ALU.add),
                reads=[("pb", ib), ("xn",), ("b2",)], writes=[("t1",)])
            ln2_feed(oc)
        w2_big(w2d, w2_evac)

        ln2_fin()

        ln0b_feed, ln0b_fin = layernorm_inc(u_sb, t1_sb, EPS0, ("u",), ("t1",))
        for oc in range(DC):
            p.step("vector", lambda v, oc=oc: v.tensor_tensor(
                out=u_sb[:, oc], in0=u_sb[:, oc], in1=cat_sb[:, oc], op=ALU.add),
                reads=[("cat",)], writes=[("u",)])
            ln0b_feed(oc)

        if tb == 0:
            dbg_dump("xmid", u_sb.ap(), ("u",))

        ln0b_fin()

        if tb == 0:
            dbg_dump("xin2", t1_sb.ap(), ("t1",))

        ib, bank = pool.take()
        for kc in range(DC):
            p.step("tensor", lambda t, kc=kc, b=bank, st=(kc == 0),
                   sp=(kc == DC - 1): t.matmul(
                b[0:NE, :], wr_sb[:, kc], t1_sb[:, kc], start=st, stop=sp),
                reads=[("wr",), ("t1",)], writes=[("pb", ib)])
        p.step("vector", lambda v, b=bank: v.tensor_scalar(
            out=lgf_sb, in0=b[0:NE, :], scalar1=br_sb[:], scalar2=None,
            op0=ALU.add),
            reads=[("pb", ib), ("br",)], writes=[("lgf",)])
        pool.give(ib)
        for gi in range(NG):
            ib, bank = pool.take()
            p.step("tensor", lambda t, gi=gi, b=bank: t.transpose(
                b[0:128, 0:NE], lgf_sb[:, gi * GRP:(gi + 1) * GRP],
                ident[0:NE, 0:NE]),
                reads=[("lgf",), ("ident",)], writes=[("pb", ib)])
            p.step("vector", lambda v, b=bank: v.tensor_copy(
                lg_sb[:, 0:NE], b[0:128, 0:NE]),
                reads=[("pb", ib)], writes=[("lg",)])
            pool.give(ib)
            p.step("vector", lambda v: v.reduce_max(
                out=mx_sb[:, 0:1], in_=lg_sb[:, 0:NE], axis=AX.X),
                reads=[("lg",)], writes=[("mx",)])
            p.step("vector", lambda v: v.tensor_scalar(
                out=lg_sb[:, NE:2 * NE], in0=lg_sb[:, 0:NE],
                scalar1=mx_sb[:, 0:1], scalar2=None, op0=ALU.is_ge),
                reads=[("lg",), ("mx",)], writes=[("eq",)])
            p.step("vector", lambda v: v.tensor_copy(
                msk_sb[:, 0:1], lg_sb[:, NE:NE + 1]),
                reads=[("eq",)], writes=[("msk",)])
            p.step("vector", lambda v: v.tensor_scalar(
                out=msk_sb[:, NE:NE + 1], in0=lg_sb[:, NE:NE + 1],
                scalar1=-1.0, scalar2=1.0, op0=ALU.mult, op1=ALU.add),
                reads=[("eq",)], writes=[("mskc",)])
            for e in range(1, NE):
                p.step("vector", lambda v, e=e: v.tensor_tensor(
                    out=msk_sb[:, e:e + 1], in0=lg_sb[:, NE + e:NE + e + 1],
                    in1=msk_sb[:, NE:NE + 1], op=ALU.mult),
                    reads=[("eq",), ("mskc",)], writes=[("msk",)])
                if e < NE - 1:
                    p.step("vector", lambda v, e=e: v.tensor_tensor(
                        out=msk_sb[:, NE:NE + 1], in0=msk_sb[:, NE:NE + 1],
                        in1=msk_sb[:, e:e + 1], op=ALU.subtract),
                        reads=[("msk",)], writes=[("mskc",)])
            for e in range(NE):
                ib, bank = pool.take()
                p.step("tensor", lambda t, e=e, b=bank: t.transpose(
                    b[0:1, 0:128], msk_sb[:, e:e + 1], ident[:]),
                    reads=[("msk",), ("ident",)], writes=[("pb", ib)])
                p.step("vector", lambda v, e=e, gi=gi, b=bank: v.tensor_copy(
                    kvz_sb[0:1, 4 + e, gi * GRP:(gi + 1) * GRP], b[0:1, 0:128]),
                    reads=[("pb", ib)], writes=[("mskrow", e)])
                pool.give(ib)

        for e in range(NE):
            ib, bank = pool.take()
            p.step("tensor", lambda t, e=e, b=bank: t.matmul(
                b[:], onesr[:], kvz_sb[0:1, 4 + e, :], start=True, stop=True),
                reads=[("mskrow", e), ("onesr",)], writes=[("pb", ib)])
            p.step("vector", lambda v, b=bank: v.tensor_copy(mbcE_sb, b[:]),
                   reads=[("pb", ib)], writes=[("mbcE",)])
            pool.give(ib)
            for par in range(2):
                p.dma("sync", f"wa{par}", wA_sb[:, par], wslab(we1d[e], par),
                      writes=[("wA", par)])
            for fc in range(FC):
                par = fc % 2
                ib, bank = pool.take()
                for kc in range(DC):
                    p.step("tensor", lambda t, par=par, kc=kc, b=bank,
                           st=(kc == 0), sp=(kc == DC - 1):
                           t.matmul(b[:], wA_sb[:, par, kc], t1_sb[:, kc],
                                    start=st, stop=sp),
                           reads=[("wA", par), ("t1",)], writes=[("pb", ib)])
                p.step("scalar", lambda s, e=e, fc=fc, b=bank: s.activation(
                    out=h1_sb[:, fc], in_=b[:], func=AF.Relu,
                    bias=be1_sb[:, e, fc:fc + 1]),
                    reads=[("pb", ib), ("be1",)], writes=[("h1",)])
                pool.give(ib)
                if fc + 2 < FC:
                    p.dma("sync", f"wa{par}", wA_sb[:, par],
                          wslab(we1d[e], fc + 2), writes=[("wA", par)])
            for oc in range(DC):
                ib, bank = w2_big(we2d[e], None, only_oc=oc)
                if e == 0:
                    p.step("vector", lambda v, e=e, oc=oc, b=bank:
                           v.scalar_tensor_tensor(
                        out=cat_sb[:, oc], in0=b[:],
                        scalar=be2_sb[:, e, oc:oc + 1], in1=mbcE_sb,
                        op0=ALU.add, op1=ALU.mult),
                        reads=[("pb", ib), ("be2",), ("mbcE",)], writes=[("cat",)])
                else:
                    p.step("vector", lambda v, e=e, oc=oc, b=bank:
                           v.scalar_tensor_tensor(
                        out=mscr_sb, in0=b[:],
                        scalar=be2_sb[:, e, oc:oc + 1], in1=mbcE_sb,
                        op0=ALU.add, op1=ALU.mult),
                        reads=[("pb", ib), ("be2",), ("mbcE",)], writes=[("mscr",)])
                    p.step("vector", lambda v, oc=oc: v.tensor_tensor(
                        out=cat_sb[:, oc], in0=cat_sb[:, oc], in1=mscr_sb,
                        op=ALU.add),
                        reads=[("mscr",)], writes=[("cat",)])
                pool.give(ib)

        for oc in range(DC):
            p.step("vector", lambda v, oc=oc: v.tensor_tensor(
                out=u_sb[:, oc], in0=u_sb[:, oc], in1=cat_sb[:, oc],
                op=ALU.add),
                reads=[("cat",)], writes=[("u",)])
        p.dma("scalar", "od",
              yT.rearrange("(c p) s -> p c s", p=128)[:, :, t0:t0 + TILE],
              u_sb[:], reads=[("u",)])

    total_od = p.cnt["od"]
    p.step("vector", lambda e: e.engine_nop(), deps=[("od", total_od)])

    p.build()
    return nc


_NC_CACHE = None


def kernel(**inputs):
    global _NC_CACHE
    g = lambda n: np.asarray(inputs[n], dtype=np.float32)
    x = g("L0_attn_x")
    wqv = np.ascontiguousarray(g("L0_attn_wq").reshape(D, D))
    wkv = np.ascontiguousarray(g("L0_attn_wk").reshape(D, D))
    wvv = np.ascontiguousarray(g("L0_attn_wv").reshape(D, D))
    wov = np.ascontiguousarray(g("L0_attn_wo"))
    omT = np.ascontiguousarray(g("L0_attn_omega").T)
    ln1g = np.ascontiguousarray(g("L0_attn_ln1_g").reshape(DC, 128).T)
    ln1b = np.ascontiguousarray(g("L0_attn_ln1_b").reshape(DC, 128).T)
    ln2g = np.ascontiguousarray(g("L0_attn_ln2_g").reshape(DC, 128).T)
    ln2b = np.ascontiguousarray(g("L0_attn_ln2_b").reshape(DC, 128).T)
    w1 = np.ascontiguousarray(g("L0_attn_w1"))
    b1 = np.ascontiguousarray(g("L0_attn_b1").reshape(FC, 128).T)
    w2 = np.ascontiguousarray(g("L0_attn_w2"))
    b2 = np.ascontiguousarray(g("L0_attn_b2").reshape(DC, 128).T)
    wr = np.ascontiguousarray(g("L0_ffn_Wr"))
    br = np.ascontiguousarray(g("L0_ffn_br").reshape(NE, 1))
    we1 = np.ascontiguousarray(g("L0_ffn_We1"))
    be1 = np.ascontiguousarray(g("L0_ffn_be1").reshape(NE, FC, 128).transpose(2, 0, 1))
    we2 = np.ascontiguousarray(g("L0_ffn_We2"))
    be2 = np.ascontiguousarray(g("L0_ffn_be2").reshape(NE, DC, 128).transpose(2, 0, 1))

    if _NC_CACHE is None:
        _NC_CACHE = build_core_program()
    nc = _NC_CACHE

    shared = dict(wq=wqv, wk=wkv, wv=wvv, wo=wov, omegaT=omT,
                  ln1_g=ln1g, ln1_b=ln1b, ln2_g=ln2g, ln2_b=ln2b,
                  w1=w1, b1=b1, w2=w2, b2=b2, wr=wr, br=br,
                  we1=we1, be1=be1, we2=we2, be2=be2)
    in_maps = []
    xTs = [np.ascontiguousarray(x[b].T) for b in range(B)]
    for c in range(8):
        b, half = c // 2, c % 2
        m = dict(shared)
        m["xTo"] = np.ascontiguousarray(xTs[b][:, half * T_OWN:(half + 1) * T_OWN])
        in_maps.append(m)

    res = run_bass_kernel_spmd(nc, in_maps, core_ids=list(range(8)))
    y = np.empty((B, S, D), dtype=np.float32)
    for c in range(8):
        b, half = c // 2, c % 2
        y[b, half * T_OWN:(half + 1) * T_OWN, :] = np.asarray(
            res.results[c]["yT"], dtype=np.float32).T
    return y


# version 07
# speedup vs baseline: 1.0728x; 1.0728x over previous
import numpy as np
import concourse.bass as bass
import concourse.mybir as mybir
from concourse.bass_utils import run_bass_kernel_spmd

F32 = mybir.dt.float32
AF = mybir.ActivationFunctionType
ALU = mybir.AluOpType
AX = mybir.AxisListType

B, S, D = 4, 8192, 1024
H, K, M = 16, 64, 256
FF = 4096
NE = 4
DC = D // 128
FC = FF // 128
TILE = 512
GRP = 128
NG = TILE // GRP
T_OWN = S // 2
NT_A = S // TILE
NT_B = T_OWN // TILE
EPS0 = 1e-5
EPS1 = 1e-6
SCALE = float(np.float32(float(K) ** (-0.25)))
LN16 = float(np.log(16.0))

ENGINES = ("tensor", "scalar", "vector", "gpsimd", "sync")
DMA_SEMS = ("wc", "wa0", "wa1", "wa2", "wa3", "wb0", "wb1", "xd", "od")


class Plan:

    def __init__(self, nc):
        self.nc = nc
        self.sems = {e: nc.alloc_semaphore(name=f"sem_{e}") for e in ENGINES}
        for s in DMA_SEMS:
            self.sems[s] = nc.alloc_semaphore(name=f"sem_{s}")
        self.cnt = {k: 0 for k in self.sems}
        self.ops = {e: [] for e in ENGINES}
        self.writer = {}
        self.reader = {}

    def step(self, eng, fn, reads=(), writes=(), deps=(), sem=None):
        sem = sem or eng
        amt = 16 if sem in DMA_SEMS else 1
        dep = {}
        def add(ev):
            if ev is None:
                return
            s, v = ev
            if dep.get(s, -1) < v:
                dep[s] = v
        for ev in deps:
            add(ev)
        for r in reads:
            for ev in self.writer.get(r, {}).items():
                add(ev)
        for w in writes:
            for ev in self.writer.get(w, {}).items():
                add(ev)
            for ev in self.reader.get(w, {}).items():
                add(ev)
        if eng != "gpsimd" and eng in dep and self.cnt[eng] - dep[eng] >= 3:
            del dep[eng]
        self.cnt[sem] += amt
        ev = (sem, self.cnt[sem])
        self.ops[eng].append((fn, sorted(dep.items()), sem, ev[1]))
        for r in reads:
            self.reader.setdefault(r, {})[sem] = ev[1]
        for w in writes:
            self.writer[w] = {sem: ev[1]}
            self.reader[w] = {}
        return ev

    def alias(self, newkey, oldkeys):
        w = dict(self.writer.get(newkey, {}))
        r = dict(self.reader.get(newkey, {}))
        for ok in oldkeys:
            for s2, v in self.writer.get(ok, {}).items():
                w[s2] = max(w.get(s2, -1), v)
            for s2, v in self.reader.get(ok, {}).items():
                r[s2] = max(r.get(s2, -1), v)
        self.writer[newkey] = w
        self.reader[newkey] = r

    def dma(self, eng, sem, dst, src, reads=(), writes=(), deps=()):
        prev = getattr(self, "_last_dma", {}).get(sem)
        if not hasattr(self, "_last_dma"):
            self._last_dma = {}
        ev = self.step(eng, lambda e, dst=dst, src=src: e.dma_start(out=dst, in_=src),
                       reads=reads, writes=writes,
                       deps=list(deps) + ([prev] if prev else []), sem=sem)
        self._last_dma[sem] = ev
        return ev

    def build(self):
        nc = self.nc
        sems = self.sems
        with nc.Block() as block:
            for eng in ENGINES:
                oplist = self.ops[eng]
                def make(oplist=oplist, eng=eng):
                    def emit(e):
                        waited = {}
                        for fn, deps, sem, val in oplist:
                            for dsem, dval in deps:
                                if waited.get(dsem, -1) >= dval:
                                    continue
                                e.wait_ge(sems[dsem], dval)
                                waited[dsem] = dval
                            instr = fn(e)
                            instr.then_inc(sems[sem], 16 if sem in DMA_SEMS else 1)
                    return emit
                getattr(block, eng)(make())


def build_core_program(debug=False):
    nc = bass.Bass("TRN2", target_bir_lowering=False, detect_race_conditions=False)
    p = Plan(nc)

    dT = lambda n, s, k="ExternalInput", dt=F32: nc.dram_tensor(n, s, dt, kind=k).ap()
    xTo = dT("xTo", [D, T_OWN])
    wq = dT("wq", [D, D]); wk = dT("wk", [D, D]); wv = dT("wv", [D, D])
    wo = dT("wo", [D, D])
    omegaT = dT("omegaT", [K, M])
    ln1_g = dT("ln1_g", [128, DC]); ln1_b = dT("ln1_b", [128, DC])
    ln2_g = dT("ln2_g", [128, DC]); ln2_b = dT("ln2_b", [128, DC])
    w1d = dT("w1", [D, FF]); b1d = dT("b1", [128, FC])
    w2d = dT("w2", [FF, D]); b2d = dT("b2", [128, DC])
    wrd = dT("wr", [D, NE]); brd = dT("br", [NE, 1])
    we1d = dT("we1", [NE, D, FF]); be1d = dT("be1", [128, NE, FC])
    we2d = dT("we2", [NE, FF, D]); be2d = dT("be2", [128, NE, DC])
    yT = dT("yT", [D, T_OWN], k="ExternalOutput")
    kvb_in = nc.dram_tensor("kvb_in", [K + 1, 8 * TILE], F32, kind="Internal").ap()
    kvb_out = nc.dram_tensor("kvb_out", [K + 1, 8 * TILE], F32, kind="Internal").ap()

    sb = lambda n, s, dt=F32: nc.alloc_sbuf_tensor(n, s, dt)
    xn_sb = sb("xn_sb", [128, DC, TILE])
    u_sb = sb("u_sb", [128, DC, TILE])
    t1_sb = sb("t1_sb", [128, DC, TILE])
    cat_sb = sb("cat_sb", [128, DC, TILE])
    h1_sb = sb("h1_sb", [128, FC, TILE])
    x2scr = sb("x2scr", [128, 2, TILE])
    sq_sb = sb("sq_sb", [128, TILE])
    sqT_sb = sb("sqT_sb", [128, NG, 16])
    stab_sb = sb("stab_sb", [128, 2, 2])
    bias_sb = sb("bias_sb", [128, 2, 2])
    pk_sb = sb("pk_sb", [128, 2, TILE])
    den_sb = sb("den_sb", [128, 2, 2])
    vzbuf = sb("vzbuf", [128, NG * H * (K + 1)])
    kvzbuf = sb("kvzbuf", [128, 8 * TILE])
    kvM_sb = sb("kvM_sb", [128, H, 2, K])
    zbcb_sb = sb("zbcb_sb", [128, 8, M])
    stats = sb("stats", [1, 2, TILE])
    lg_sb = sb("lg_sb", [128, 2 * NE])
    mx_sb = sb("mx_sb", [128, 1])
    msk_sb = sb("msk_sb", [128, NE + 1])
    wA_sb = sb("wA_sb", [128, 2, DC, 128])
    wB_sb = sb("wB_sb", [128, 2, 4, 128])
    wr_sb = sb("wr_sb", [128, DC, NE])
    ob_sb = sb("ob_sb", [128, 2 * M])
    ident = sb("ident", [128, 128])
    ones1 = sb("ones1", [128, 1])
    onesr = sb("onesr", [1, 128])
    blk_sb = sb("blk_sb", [128, DC, 16])
    gb_sb = sb("gb_sb", [128, 4 * DC])
    b1_sb = sb("b1_sb", [128, FC])
    b2_sb = sb("b2_sb", [128, DC])
    br_sb = sb("br_sb", [NE, 1])
    be1_sb = sb("be1_sb", [128, NE, FC])
    be2_sb = sb("be2_sb", [128, NE, DC])
    epsc_sb = sb("epsc_sb", [1, 2])

    vz_sb = vzbuf.ap().rearrange("p (g h k) -> p g h k", g=NG, h=H)
    pqT_sb = vzbuf.ap()[:, 0:4 * TILE].rearrange("p (c t) -> p c t", c=4)
    zbca_sb = vzbuf.ap()[:, 4 * TILE:4 * TILE + 8 * M].rearrange(
        "p (h m) -> p h m", h=8)
    zbc_ap = lambda h: zbca_sb[:, h] if h < 8 else zbcb_sb[:, h - 8]
    kvz_sb = kvzbuf.ap().rearrange("p (r t) -> p r t", r=8)
    mbcE_sb = kvzbuf.ap()[:, 0:TILE]
    mscr_sb = kvzbuf.ap()[:, TILE:2 * TILE]
    lgf_sb = kvzbuf.ap()[0:NE, 2 * TILE:3 * TILE]
    zrow_sb = x2scr.ap()[0:1, 0, M:2 * M]
    pq_ap = lambda par: kvzbuf.ap()[:, 3 * TILE + par * M:3 * TILE + (par + 1) * M]

    pb = [nc.alloc_psum_tensor(f"pb{i}", [128, TILE], F32) for i in range(8)]

    class Pool:
        def __init__(self):
            self.order = list(range(8))
        def take(self):
            i = self.order.pop(0)
            return i, pb[i]
        def give(self, i):
            self.order.append(i)
    pool = Pool()

    p.step("gpsimd", lambda g: g.memset(ident[:], 0.0), writes=[("ident0",)])
    p.step("gpsimd", lambda g: g.affine_select(
        out=ident[:], in_=ident[:], compare_op=ALU.not_equal,
        fill=1.0, base=0, pattern=[[-1, 128]], channel_multiplier=1),
        reads=[("ident0",)], writes=[("ident",)])
    p.step("gpsimd", lambda g: g.memset(blk_sb[:], 0.0), writes=[("blk0",)])
    def blk_fill(g):
        for c in range(DC):
            g.memset(blk_sb[0:64, c, 2 * c:2 * c + 1], 0.5)
            instr = g.memset(blk_sb[64:128, c, 2 * c + 1:2 * c + 2], 0.5)
        return instr
    p.step("gpsimd", blk_fill, reads=[("blk0",)], writes=[("blk",)])
    def misc_consts(g):
        g.memset(ones1[:], 1.0)
        g.memset(onesr[:], 1.0)
        g.memset(ob_sb[:], 0.0)
        g.memset(epsc_sb[0:1, 0:1], EPS0)
        g.memset(epsc_sb[0:1, 1:2], EPS1)
        g.memset(sq_sb[:], 0.0)
        g.memset(kvzbuf[:], 0.0)
        for h in range(H):
            for gi in range(NG):
                instr = g.memset(vz_sb[:, gi, h, K:K + 1], 1.0)
        return instr
    p.step("gpsimd", misc_consts,
           writes=[("ones1",), ("onesr",), ("ob",), ("sq",), ("kvz",),
                   ("vzco",), ("epsc",)])

    p.dma("sync", "wc", ob_sb[0:64, 0:M], omegaT, writes=[("ob",)])
    p.dma("sync", "wc", ob_sb[64:128, M:2 * M], omegaT, writes=[("ob",)])
    p.dma("sync", "wc", gb_sb[:, 0:DC], ln1_g, writes=[("gb0",)])
    p.dma("sync", "wc", gb_sb[:, DC:2 * DC], ln1_b, writes=[("gb1",)])
    p.dma("sync", "wc", gb_sb[:, 2 * DC:3 * DC], ln2_g, writes=[("gb2",)])
    p.dma("sync", "wc", gb_sb[:, 3 * DC:4 * DC], ln2_b, writes=[("gb3",)])
    p.dma("sync", "wc", b1_sb[:], b1d, writes=[("b1",)])
    p.dma("sync", "wc", b2_sb[:], b2d, writes=[("b2",)])
    p.dma("sync", "wc", br_sb[:], brd, writes=[("br",)])
    p.dma("sync", "wc", be1_sb[:], be1d, writes=[("be1",)])
    p.dma("sync", "wc", be2_sb[:], be2d, writes=[("be2",)])
    p.dma("sync", "wc", wr_sb[:], wrd.rearrange("(kc p) e -> p kc e", p=128),
          writes=[("wr",)])

    LN1G, LN1B = gb_sb[:, 0:DC], gb_sb[:, DC:2 * DC]
    LN2G, LN2B = gb_sb[:, 2 * DC:3 * DC], gb_sb[:, 3 * DC:4 * DC]

    wslab = lambda w, oc: w.rearrange("(kc p) (oc f) -> p oc kc f", p=128, f=128)[:, oc]

    def layernorm_inc(src, dst, eps, skey, dkey, affine=None, affine_keys=()):
        ib1, bank1 = pool.take()
        ib2, bank2 = pool.take()

        def feed(c):
            ci = c % 2
            p.step("scalar", lambda s, c=c, ci=ci: s.activation(
                out=x2scr[:, ci], in_=src[:, c], func=AF.Square),
                reads=[skey], writes=[("x2", ci)])
            p.step("tensor", lambda t, c=c, b=bank1, st=(c == 0), sp=(c == DC - 1):
                   t.matmul(b[0:1, :], ones1[:], src[:, c], start=st, stop=sp),
                   reads=[skey, ("ones1",)], writes=[("pb", ib1)])
            p.step("tensor", lambda t, c=c, ci=ci, b=bank2, st=(c == 0), sp=(c == DC - 1):
                   t.matmul(b[0:1, :], ones1[:], x2scr[:, ci], start=st, stop=sp),
                   reads=[("x2", ci), ("ones1",)], writes=[("pb", ib2)])

        def finish():
            invD = 1.0 / D
            r1, r2 = bank1[0:1, :], bank2[0:1, :]
            p.step("vector", lambda v: v.tensor_scalar(
                out=stats[:, 0], in0=r1, scalar1=invD, scalar2=None, op0=ALU.mult),
                reads=[("pb", ib1)], writes=[("st0",)])
            p.step("scalar", lambda s: s.activation(
                out=stats[:, 1], in_=stats[:, 0], func=AF.Square),
                reads=[("st0",)], writes=[("st1",)])
            p.step("vector", lambda v: v.scalar_tensor_tensor(
                out=r2, in0=r2, scalar=invD, in1=stats[:, 1],
                op0=ALU.mult, op1=ALU.subtract),
                reads=[("st1",)], writes=[("pb", ib2)])
            eps_ap = epsc_sb[0:1, (0 if eps == EPS0 else 1):(1 if eps == EPS0 else 2)]
            p.step("scalar", lambda s, eps_ap=eps_ap: s.activation(
                out=r2, in_=r2, func=AF.Sqrt, bias=eps_ap),
                reads=[("epsc",)], writes=[("pb", ib2)])
            p.step("vector", lambda v: v.reciprocal(out=r2, in_=r2),
                   reads=[], writes=[("pb", ib2)])
            p.step("vector", lambda v: v.tensor_copy(stats[:, 1], r2),
                   reads=[("pb", ib2)], writes=[("st1",)])
            p.step("vector", lambda v: v.scalar_tensor_tensor(
                out=stats[:, 0], in0=stats[:, 0], scalar=-1.0, in1=stats[:, 1],
                op0=ALU.mult, op1=ALU.mult),
                reads=[("st1",)], writes=[("st0",)])
            pool.give(ib1); pool.give(ib2)
            ib3, bank3 = pool.take()
            ib4, bank4 = pool.take()
            p.step("tensor", lambda t, b=bank3: t.matmul(
                b[:], onesr[:], stats[:, 1], start=True, stop=True),
                reads=[("st1",), ("onesr",)], writes=[("pb", ib3)])
            p.step("tensor", lambda t, b=bank4: t.matmul(
                b[:], onesr[:], stats[:, 0], start=True, stop=True),
                reads=[("st0",), ("onesr",)], writes=[("pb", ib4)])
            for c in range(DC):
                p.step("vector", lambda v, c=c, b=bank3: v.tensor_tensor(
                    out=dst[:, c], in0=src[:, c], in1=b[:], op=ALU.mult),
                    reads=[skey, ("pb", ib3)], writes=[dkey])
                p.step("vector", lambda v, c=c, b=bank4: v.tensor_tensor(
                    out=dst[:, c], in0=dst[:, c], in1=b[:], op=ALU.add),
                    reads=[("pb", ib4)], writes=[dkey])
                if affine is not None:
                    g_ap, b_ap = affine
                    p.step("vector", lambda v, c=c: v.tensor_scalar(
                        out=dst[:, c], in0=dst[:, c], scalar1=g_ap[:, c:c + 1],
                        scalar2=b_ap[:, c:c + 1], op0=ALU.mult, op1=ALU.add),
                        reads=list(affine_keys), writes=[dkey])
            pool.give(ib3); pool.give(ib4)

        return feed, finish

    def layernorm(src, dst, eps, skey, dkey, affine=None, affine_keys=()):
        ib1, bank1 = pool.take()
        ib2, bank2 = pool.take()
        for c in range(DC):
            ci = c % 2
            p.step("scalar", lambda s, c=c, ci=ci: s.activation(
                out=x2scr[:, ci], in_=src[:, c], func=AF.Square),
                reads=[skey], writes=[("x2", ci)])
            p.step("tensor", lambda t, c=c, b=bank1, st=(c == 0), sp=(c == DC - 1):
                   t.matmul(b[0:1, :], ones1[:], src[:, c], start=st, stop=sp),
                   reads=[skey, ("ones1",)], writes=[("pb", ib1)])
            p.step("tensor", lambda t, c=c, ci=ci, b=bank2, st=(c == 0), sp=(c == DC - 1):
                   t.matmul(b[0:1, :], ones1[:], x2scr[:, ci], start=st, stop=sp),
                   reads=[("x2", ci), ("ones1",)], writes=[("pb", ib2)])
        invD = 1.0 / D
        r1, r2 = bank1[0:1, :], bank2[0:1, :]
        p.step("vector", lambda v: v.tensor_scalar(
            out=stats[:, 0], in0=r1, scalar1=invD, scalar2=None, op0=ALU.mult),
            reads=[("pb", ib1)], writes=[("st0",)])
        p.step("scalar", lambda s: s.activation(
            out=stats[:, 1], in_=stats[:, 0], func=AF.Square),
            reads=[("st0",)], writes=[("st1",)])
        p.step("vector", lambda v: v.scalar_tensor_tensor(
            out=r2, in0=r2, scalar=invD, in1=stats[:, 1],
            op0=ALU.mult, op1=ALU.subtract),
            reads=[("st1",)], writes=[("pb", ib2)])
        eps_ap = epsc_sb[0:1, (0 if eps == EPS0 else 1):(1 if eps == EPS0 else 2)]
        p.step("scalar", lambda s, eps_ap=eps_ap: s.activation(
            out=r2, in_=r2, func=AF.Sqrt, bias=eps_ap),
            reads=[("epsc",)], writes=[("pb", ib2)])
        p.step("vector", lambda v: v.reciprocal(out=r2, in_=r2),
               reads=[], writes=[("pb", ib2)])
        p.step("vector", lambda v: v.tensor_copy(stats[:, 1], r2),
               reads=[("pb", ib2)], writes=[("st1",)])
        p.step("vector", lambda v: v.scalar_tensor_tensor(
            out=stats[:, 0], in0=stats[:, 0], scalar=-1.0, in1=stats[:, 1],
            op0=ALU.mult, op1=ALU.mult),
            reads=[("st1",)], writes=[("st0",)])
        pool.give(ib1); pool.give(ib2)
        ib3, bank3 = pool.take()
        ib4, bank4 = pool.take()
        p.step("tensor", lambda t, b=bank3: t.matmul(
            b[:], onesr[:], stats[:, 1], start=True, stop=True),
            reads=[("st1",), ("onesr",)], writes=[("pb", ib3)])
        p.step("tensor", lambda t, b=bank4: t.matmul(
            b[:], onesr[:], stats[:, 0], start=True, stop=True),
            reads=[("st0",), ("onesr",)], writes=[("pb", ib4)])
        for c in range(DC):
            p.step("vector", lambda v, c=c, b=bank3: v.tensor_tensor(
                out=dst[:, c], in0=src[:, c], in1=b[:], op=ALU.mult),
                reads=[skey, ("pb", ib3)], writes=[dkey])
            p.step("vector", lambda v, c=c, b=bank4: v.tensor_tensor(
                out=dst[:, c], in0=dst[:, c], in1=b[:], op=ALU.add),
                reads=[("pb", ib4)], writes=[dkey])
            if affine is not None:
                g_ap, b_ap = affine
                p.step("vector", lambda v, c=c: v.tensor_scalar(
                    out=dst[:, c], in0=dst[:, c], scalar1=g_ap[:, c:c + 1],
                    scalar2=b_ap[:, c:c + 1], op0=ALU.mult, op1=ALU.add),
                    reads=list(affine_keys), writes=[dkey])
        pool.give(ib3); pool.give(ib4)

    def qk_proj(w_dram, skey):
        sqi, sqb = pool.take()
        for par in range(2):
            p.dma("sync", f"wa{par}", wA_sb[:, par], wslab(w_dram, par),
                  writes=[("wA", par)])
        for oc in range(DC):
            par = oc % 2
            ib, bank = pool.take()
            for kc in range(DC):
                p.step("tensor", lambda t, par=par, kc=kc, b=bank,
                       st=(kc == 0), sp=(kc == DC - 1):
                       t.matmul(b[:], wA_sb[:, par, kc], xn_sb[:, kc],
                                start=st, stop=sp),
                       reads=[("wA", par), skey], writes=[("pb", ib)])
            p.step("scalar", lambda s, oc=oc, b=bank: s.activation(
                out=u_sb[:, oc], in_=b[:], func=AF.Copy, scale=SCALE),
                reads=[("pb", ib)], writes=[("u",)])
            p.step("scalar", lambda s, oc=oc, b=bank, ci=oc % 2: s.activation(
                out=x2scr[:, ci], in_=b[:], func=AF.Square, scale=SCALE),
                reads=[("pb", ib)], writes=[("x2", oc % 2)])
            p.step("tensor", lambda t, oc=oc, ci=oc % 2, st=(oc == 0),
                   sp=(oc == DC - 1), b=sqb:
                   t.matmul(b[0:16, :], blk_sb[:, oc], x2scr[:, ci],
                            start=st, stop=sp),
                   reads=[("x2", oc % 2), ("blk",)], writes=[("pb", sqi)])
            pool.give(ib)
            if oc + 2 < DC:
                p.dma("sync", f"wa{par}", wA_sb[:, par], wslab(w_dram, oc + 2),
                      writes=[("wA", par)])
        p.step("vector", lambda v, b=sqb: v.tensor_copy(sq_sb[0:16, :], b[0:16, :]),
               reads=[("pb", sqi)], writes=[("sq",)])
        pool.give(sqi)
        for gi in range(NG):
            ib, bank = pool.take()
            p.step("tensor", lambda t, gi=gi, b=bank: t.transpose(
                b[:, 0:128], sq_sb[:, gi * GRP:(gi + 1) * GRP], ident[:]),
                reads=[("sq",), ("ident",)], writes=[("pb", ib)])
            p.step("vector", lambda v, gi=gi, b=bank: v.tensor_copy(
                sqT_sb[:, gi], b[:, 0:16]),
                reads=[("pb", ib)], writes=[("sqT",)])
            pool.give(ib)

    w4bufs = [wA_sb.ap()[:, 0, 0:4], wA_sb.ap()[:, 0, 4:8],
              wA_sb.ap()[:, 1, 0:4], wA_sb.ap()[:, 1, 4:8],
              wB_sb.ap()[:, 0], wB_sb.ap()[:, 1]]
    w4sems = ["wa0", "wa1", "wa2", "wa3", "wb0", "wb1"]

    def w2_alias_in():
        p.alias(("w4", 0), [("wA", 0)])
        p.alias(("w4", 1), [("wA", 0)])
        p.alias(("w4", 2), [("wA", 1)])
        p.alias(("w4", 3), [("wA", 1)])

    def w2_alias_out():
        p.alias(("wA", 0), [("w4", 0), ("w4", 1)])
        p.alias(("wA", 1), [("w4", 2), ("w4", 3)])

    def w2_big(wd, evac_cb, only_oc=None):
        w2_alias_in()
        ocs = range(DC) if only_oc is None else [only_oc]
        ret = None
        for oc in ocs:
            ib, bank = pool.take()
            for j in range(8):
                q = (oc * 8 + j) % 6
                p.dma("sync", w4sems[q], w4bufs[q],
                      wslab(wd, oc).rearrange("p (j kc) f -> p j kc f", j=8)[:, j],
                      writes=[("w4", q)])
                for kc4 in range(4):
                    kc = j * 4 + kc4
                    p.step("tensor", lambda t, q=q, kc4=kc4, kc=kc, b=bank,
                           st=(kc == 0), sp=(kc == FC - 1):
                           t.matmul(b[:], w4bufs[q][:, kc4], h1_sb[:, kc],
                                    start=st, stop=sp),
                           reads=[("w4", q), ("h1",)], writes=[("pb", ib)])
            if evac_cb is None:
                ret = (ib, bank)
            else:
                evac_cb(oc, bank, ib)
                pool.give(ib)
        w2_alias_out()
        return ret

    def phi_exp(pr, gi):
        gp = gi % 2
        ib, bank = pool.take()
        p.step("tensor", lambda t, pr=pr, gi=gi, b=bank: t.matmul(
            b[:], u_sb[:, pr, gi * GRP:(gi + 1) * GRP], ob_sb[:],
            start=True, stop=True),
            reads=[("u",), ("ob",)], writes=[("pb", ib)])
        p.step("vector", lambda v, b=bank, gp=gp: v.reduce_max(
            out=stab_sb[:, gp, 0:1], in_=b[:, 0:M], axis=AX.X),
            reads=[("pb", ib)], writes=[("stab", gp)])
        p.step("vector", lambda v, b=bank, gp=gp: v.reduce_max(
            out=stab_sb[:, gp, 1:2], in_=b[:, M:2 * M], axis=AX.X),
            reads=[("pb", ib)], writes=[("stab", gp)])
        p.step("vector", lambda v, pr=pr, gi=gi, gp=gp: v.tensor_tensor(
            out=bias_sb[:, gp], in0=sqT_sb[:, gi, 2 * pr:2 * pr + 2],
            in1=stab_sb[:, gp], op=ALU.add),
            reads=[("sqT",), ("stab", gp)], writes=[("bias", gp)])
        p.step("vector", lambda v, gp=gp: v.tensor_scalar(
            out=bias_sb[:, gp], in0=bias_sb[:, gp], scalar1=-1.0, scalar2=-LN16,
            op0=ALU.mult, op1=ALU.add),
            reads=[], writes=[("bias", gp)])
        for hh in range(2):
            p.step("scalar", lambda s, hh=hh, b=bank, gp=gp: s.activation(
                out=pk_sb[:, gp, hh * M:(hh + 1) * M],
                in_=b[:, hh * M:(hh + 1) * M],
                func=AF.Exp, bias=bias_sb[:, gp, hh:hh + 1]),
                reads=[("pb", ib), ("bias", gp)], writes=[("pk", gp)])
        pool.give(ib)

    xo_t = lambda t0: xTo.rearrange("(c p) s -> p c s", p=128)[:, :, t0:t0 + TILE]
    p.dma("sync", "xd", cat_sb[:], xo_t(0), writes=[("cat",)])
    layernorm(cat_sb, xn_sb, EPS0, ("cat",), ("xn",))
    for ta in range(NT_B):
        t0 = ta * TILE
        qk_proj(wk, ("xn",))

        wAq = wA_sb.ap().rearrange("p a b c -> p (a b c)").rearrange(
            "p (kc f) -> p kc f", kc=DC)
        for q in range(4):
            p.dma("sync", "wa0",
                  wAq, wv.rearrange("(kc p) (q f) -> p q kc f", p=128, f=256)[:, q],
                  writes=[("wA", 0), ("wA", 1)])
            for gi in range(NG):
                ib, bank = pool.take()
                for kc in range(DC):
                    p.step("tensor", lambda t, kc=kc, gi=gi, b=bank,
                           st=(kc == 0), sp=(kc == DC - 1):
                           t.matmul(b[:, 0:256],
                                    xn_sb[:, kc, gi * GRP:(gi + 1) * GRP],
                                    wAq[:, kc], start=st, stop=sp),
                           reads=[("wA", 0), ("wA", 1), ("xn",)],
                           writes=[("pb", ib)])
                for j in range(4):
                    h = 4 * q + j
                    p.step("vector", lambda v, gi=gi, h=h, j=j, b=bank: v.tensor_copy(
                        vz_sb[:, gi, h, 0:K], b[:, j * K:(j + 1) * K]),
                        reads=[("pb", ib)], writes=[("vz", gi, h)])
                pool.give(ib)

        if ta + 1 < NT_B:
            p.dma("sync", "xd", cat_sb[:], xo_t(t0 + TILE), writes=[("cat",)])
            layernorm(cat_sb, xn_sb, EPS0, ("cat",), ("xn",))

        for pr in range(8):
            kvi, kvb = pool.take()
            phi_exp(pr, 0)
            p.step("vector", lambda v: v.tensor_scalar(
                out=pk_sb[:, 0], in0=pk_sb[:, 0], scalar1=1e-6, scalar2=None,
                op0=ALU.add), reads=[], writes=[("pk", 0)])
            for gi in range(NG):
                gp = gi % 2
                if gi + 1 < NG:
                    phi_exp(pr, gi + 1)
                    p.step("vector", lambda v, gp2=(gi + 1) % 2: v.tensor_scalar(
                        out=pk_sb[:, gp2], in0=pk_sb[:, gp2], scalar1=1e-6,
                        scalar2=None, op0=ALU.add),
                        reads=[], writes=[("pk", (gi + 1) % 2)])
                for hh in range(2):
                    h = 2 * pr + hh
                    st = (gi == 0 and hh == 0)
                    sp = (gi == NG - 1 and hh == 1)
                    p.step("tensor", lambda t, h=h, hh=hh, gi=gi, st=st, sp=sp,
                           gp=gp, b=kvb: t.matmul(
                        b[0:K + 1, hh * M:(hh + 1) * M], vz_sb[:, gi, h],
                        pk_sb[:, gp, hh * M:(hh + 1) * M], start=st, stop=sp),
                        reads=[("pk", gp), ("vz", gi, h), ("vzco",)],
                        writes=[("pb", kvi)])
            p.step("vector", lambda v, pr=pr, b=kvb: v.tensor_tensor(
                out=kvz_sb[0:K + 1, pr], in0=kvz_sb[0:K + 1, pr],
                in1=b[0:K + 1, :], op=ALU.add),
                reads=[("pb", kvi)], writes=[("kvz",)])
            pool.give(kvi)

    for tb in range(NT_B):
        t0 = tb * TILE
        p.dma("sync", "xd", cat_sb[:], xo_t(t0), writes=[("cat",)])
        layernorm(cat_sb, xn_sb, EPS0, ("cat",), ("xn",))
        qk_proj(wq, ("xn",))

        if tb == 0:
                p.dma("sync", "xd", kvb_in, kvzbuf.ap()[0:K + 1, 0:8 * TILE],
                      reads=[("kvz",)], writes=[("kvbi",)])
                p.step("gpsimd", lambda g: g.collective_compute(
                    "AllReduce", ALU.add,
                    replica_groups=[[0, 1], [2, 3], [4, 5], [6, 7]],
                    ins=[kvb_in.opt()], outs=[kvb_out.opt()]),
                    reads=[("kvbi",)], writes=[("kvbo",)])
                p.dma("sync", "xd", kvzbuf.ap()[0:K + 1, 0:8 * TILE], kvb_out,
                      reads=[("kvbo",)], writes=[("kvz",)])

                p.alias(("zrow",), [("x2", 0)])
                p.alias(("zbc",), [("vz", gi, h) for gi in range(NG) for h in range(H)]
                        + [("vzco",)])
                for pr in range(8):
                    for hh in range(2):
                        h = 2 * pr + hh
                        for mc in range(2):
                            ib, bank = pool.take()
                            p.step("tensor", lambda t, pr=pr, hh=hh, mc=mc, b=bank: t.transpose(
                                b[:, 0:128],
                                kvz_sb[:, pr, hh * M + mc * 128:hh * M + (mc + 1) * 128],
                                ident[:]),
                                reads=[("kvz",), ("ident",)], writes=[("pb", ib)])
                            p.step("vector", lambda v, h=h, mc=mc, b=bank: v.tensor_copy(
                                kvM_sb[:, h, mc], b[:, 0:K]),
                                reads=[("pb", ib)], writes=[("kvM",)])
                            p.step("vector", lambda v, b=bank: v.tensor_copy(
                                mx_sb[:, 0:1], b[:, K:K + 1]),
                                reads=[("pb", ib)], writes=[("zT",)])
                            pool.give(ib)
                            ib2, bank2 = pool.take()
                            p.step("tensor", lambda t, b2=bank2: t.transpose(
                                b2[0:1, 0:128], mx_sb[:, 0:1], ident[:]),
                                reads=[("zT",), ("ident",)], writes=[("pb", ib2)])
                            p.step("vector", lambda v, mc=mc, b2=bank2: v.tensor_copy(
                                zrow_sb[:, mc * 128:(mc + 1) * 128], b2[0:1, 0:128]),
                                reads=[("pb", ib2)], writes=[("zrow",)])
                            pool.give(ib2)
                        ib3, bank3 = pool.take()
                        p.step("tensor", lambda t, b=bank3: t.matmul(
                            b[:, 0:M], onesr[:], zrow_sb[:], start=True, stop=True),
                            reads=[("zrow",), ("onesr",)], writes=[("pb", ib3)])
                        p.step("vector", lambda v, h=h, b=bank3: v.tensor_copy(
                            zbc_ap(h), b[:, 0:M]),
                            reads=[("pb", ib3)], writes=[("zbc",)])
                        pool.give(ib3)

                dbg_outs = []
                def dbg_dump(tag, ap_sb, rkey):
                    if not debug:
                        return
                    dt_ = nc.dram_tensor(f"dbg_{tag}", list(ap_sb.shape), F32,
                                         kind="ExternalOutput").ap()
                    p.dma("sync", "od", dt_, ap_sb, reads=[rkey])
                    dbg_outs.append(tag)

                dbg_dump("kvM", kvM_sb.ap(), ("kvM",))
                dbg_dump("zbca", zbca_sb, ("zbc",))
                dbg_dump("zbcb", zbcb_sb.ap(), ("zbc",))

                p.alias(("mbcE",), [("kvz",)])
                p.alias(("mscr",), [("kvz",)])
                p.alias(("lgf",), [("kvz",)])
                for e in range(NE):
                    p.alias(("mskrow", e), [("kvz",)])
                p.alias(("mx",), [("zT",)])
                p.alias(("pq", 0), [("kvz",)])
                p.alias(("pq", 1), [("kvz",)])
                p.alias(("x2", 0), [("zrow",)])
                vz_keys = [("vz", gi, h) for gi in range(NG) for h in range(H)] + \
                    [("vzco",), ("zbc",)]
                for c in range(4):
                    p.alias(("pqT", c), vz_keys)




        def pq_chain(pr, gi):
            gp = gi % 2
            for hh in range(2):
                h = 2 * pr + hh
                p.step("vector", lambda v, hh=hh, gp=gp: v.tensor_scalar(
                    out=pq_ap(hh), in0=pk_sb[:, gp, hh * M:(hh + 1) * M],
                    scalar1=1e-6, scalar2=None, op0=ALU.add),
                    reads=[("pk", gp)], writes=[("pq", hh)])
                p.step("vector", lambda v, h=h, hh=hh: v.scalar_tensor_tensor(
                    out=x2scr[:, hh, 0:M], in0=pq_ap(hh), scalar=1.0,
                    in1=zbc_ap(h), op0=ALU.mult, op1=ALU.mult,
                    accum_out=den_sb[:, hh, 0:1]),
                    reads=[("pq", hh), ("zbc",)],
                    writes=[("x2", hh), ("den", hh)])
                p.step("vector", lambda v, hh=hh: v.reciprocal(
                    out=den_sb[:, hh, 1:2], in_=den_sb[:, hh, 0:1]),
                    reads=[], writes=[("den", hh)])
                p.step("vector", lambda v, hh=hh: v.tensor_scalar(
                    out=pq_ap(hh), in0=pq_ap(hh), scalar1=den_sb[:, hh, 1:2],
                    scalar2=None, op0=ALU.mult),
                    reads=[("den", hh)], writes=[("pq", hh)])
                for mc in range(2):
                    ib, bank = pool.take()
                    p.step("tensor", lambda t, mc=mc, hh=hh, b=bank: t.transpose(
                        b[:, 0:128], pq_ap(hh)[:, mc * 128:(mc + 1) * 128],
                        ident[:]),
                        reads=[("pq", hh), ("ident",)], writes=[("pb", ib)])
                    slot = 2 * hh + mc
                    p.step("vector", lambda v, slot=slot, gi=gi, b=bank:
                           v.tensor_copy(
                        pqT_sb[:, slot, gi * GRP:(gi + 1) * GRP], b[:, 0:128]),
                        reads=[("pb", ib)], writes=[("pqT", slot)])
                    pool.give(ib)

        for pr in range(8):
            phi_exp(pr, 0)
            for gi in range(NG):
                if gi + 1 < NG:
                    phi_exp(pr, gi + 1)
                pq_chain(pr, gi)
            ib, bank = pool.take()
            for hh in range(2):
                h = 2 * pr + hh
                for mc in range(2):
                    tp = (0, 64) if hh else None
                    p.step("tensor", lambda t, h=h, hh=hh, mc=mc, b=bank,
                           st=(mc == 0), sp=(mc == 1), tp=tp: t.matmul(
                        b[64 * hh:64 * hh + 64, :], kvM_sb[:, h, mc],
                        pqT_sb[:, 2 * hh + mc], start=st, stop=sp,
                        tile_position=tp),
                        reads=[("kvM",), ("pqT", 2 * hh + mc)],
                        writes=[("pb", ib)])
            p.step("scalar", lambda s, pr=pr, b=bank: s.activation(
                out=cat_sb[:, pr], in_=b[:], func=AF.Copy),
                reads=[("pb", ib)], writes=[("cat",)])
            pool.give(ib)

        if tb == 0:
            dbg_dump("xn", xn_sb.ap(), ("xn",))
            dbg_dump("uq", u_sb.ap(), ("u",))
            dbg_dump("sq", sq_sb.ap(), ("sq",))
            dbg_dump("cat", cat_sb.ap(), ("cat",))

        ln1_feed, ln1_fin = layernorm_inc(t1_sb, xn_sb, EPS1, ("t1",), ("xn",),
                                          affine=(LN1G, LN1B),
                                          affine_keys=[("gb0",), ("gb1",)])
        for par in range(2):
            p.dma("sync", f"wa{par}", wA_sb[:, par], wslab(wo, par),
                  writes=[("wA", par)])
        for oc in range(DC):
            par = oc % 2
            ib, bank = pool.take()
            for kc in range(DC):
                p.step("tensor", lambda t, par=par, kc=kc, b=bank,
                       st=(kc == 0), sp=(kc == DC - 1):
                       t.matmul(b[:], wA_sb[:, par, kc], cat_sb[:, kc],
                                start=st, stop=sp),
                       reads=[("wA", par), ("cat",)], writes=[("pb", ib)])
            p.step("vector", lambda v, oc=oc, b=bank: v.tensor_tensor(
                out=t1_sb[:, oc], in0=b[:], in1=xn_sb[:, oc], op=ALU.add),
                reads=[("pb", ib), ("xn",)], writes=[("t1",)])
            pool.give(ib)
            ln1_feed(oc)
            if oc + 2 < DC:
                p.dma("sync", f"wa{par}", wA_sb[:, par], wslab(wo, oc + 2),
                      writes=[("wA", par)])

        if tb == 0:
            dbg_dump("t1", t1_sb.ap(), ("t1",))

        p.dma("sync", "xd", cat_sb[:],
              xTo.rearrange("(c p) s -> p c s", p=128)[:, :, t0:t0 + TILE],
              writes=[("cat",)])

        ln1_fin()

        if tb == 0:
            dbg_dump("x1", xn_sb.ap(), ("xn",))

        for par in range(2):
            p.dma("sync", f"wa{par}", wA_sb[:, par], wslab(w1d, par),
                  writes=[("wA", par)])
        for fc in range(FC):
            par = fc % 2
            ib, bank = pool.take()
            for kc in range(DC):
                p.step("tensor", lambda t, par=par, kc=kc, b=bank,
                       st=(kc == 0), sp=(kc == DC - 1):
                       t.matmul(b[:], wA_sb[:, par, kc], xn_sb[:, kc],
                                start=st, stop=sp),
                       reads=[("wA", par), ("xn",)], writes=[("pb", ib)])
            p.step("scalar", lambda s, fc=fc, b=bank: s.activation(
                out=h1_sb[:, fc], in_=b[:], func=AF.Gelu,
                bias=b1_sb[:, fc:fc + 1]),
                reads=[("pb", ib), ("b1",)], writes=[("h1",)])
            pool.give(ib)
            if fc + 2 < FC:
                p.dma("sync", f"wa{par}", wA_sb[:, par], wslab(w1d, fc + 2),
                      writes=[("wA", par)])

        if tb == 0:
            dbg_dump("h1", h1_sb.ap(), ("h1",))

        ln2_feed, ln2_fin = layernorm_inc(t1_sb, u_sb, EPS1, ("t1",), ("u",),
                                          affine=(LN2G, LN2B),
                                          affine_keys=[("gb2",), ("gb3",)])
        def w2_evac(oc, bank, ib):
            p.step("vector", lambda v, oc=oc, b=bank: v.scalar_tensor_tensor(
                out=t1_sb[:, oc], in0=b[:], scalar=b2_sb[:, oc:oc + 1],
                in1=xn_sb[:, oc], op0=ALU.add, op1=ALU.add),
                reads=[("pb", ib), ("xn",), ("b2",)], writes=[("t1",)])
            ln2_feed(oc)
        w2_big(w2d, w2_evac)

        ln2_fin()

        ln0b_feed, ln0b_fin = layernorm_inc(u_sb, t1_sb, EPS0, ("u",), ("t1",))
        for oc in range(DC):
            p.step("vector", lambda v, oc=oc: v.tensor_tensor(
                out=u_sb[:, oc], in0=u_sb[:, oc], in1=cat_sb[:, oc], op=ALU.add),
                reads=[("cat",)], writes=[("u",)])
            ln0b_feed(oc)

        if tb == 0:
            dbg_dump("xmid", u_sb.ap(), ("u",))

        ln0b_fin()

        if tb == 0:
            dbg_dump("xin2", t1_sb.ap(), ("t1",))

        ib, bank = pool.take()
        for kc in range(DC):
            p.step("tensor", lambda t, kc=kc, b=bank, st=(kc == 0),
                   sp=(kc == DC - 1): t.matmul(
                b[0:NE, :], wr_sb[:, kc], t1_sb[:, kc], start=st, stop=sp),
                reads=[("wr",), ("t1",)], writes=[("pb", ib)])
        p.step("vector", lambda v, b=bank: v.tensor_scalar(
            out=lgf_sb, in0=b[0:NE, :], scalar1=br_sb[:], scalar2=None,
            op0=ALU.add),
            reads=[("pb", ib), ("br",)], writes=[("lgf",)])
        pool.give(ib)
        for gi in range(NG):
            ib, bank = pool.take()
            p.step("tensor", lambda t, gi=gi, b=bank: t.transpose(
                b[0:128, 0:NE], lgf_sb[:, gi * GRP:(gi + 1) * GRP],
                ident[0:NE, 0:NE]),
                reads=[("lgf",), ("ident",)], writes=[("pb", ib)])
            p.step("vector", lambda v, b=bank: v.tensor_copy(
                lg_sb[:, 0:NE], b[0:128, 0:NE]),
                reads=[("pb", ib)], writes=[("lg",)])
            pool.give(ib)
            p.step("vector", lambda v: v.reduce_max(
                out=mx_sb[:, 0:1], in_=lg_sb[:, 0:NE], axis=AX.X),
                reads=[("lg",)], writes=[("mx",)])
            p.step("vector", lambda v: v.tensor_scalar(
                out=lg_sb[:, NE:2 * NE], in0=lg_sb[:, 0:NE],
                scalar1=mx_sb[:, 0:1], scalar2=None, op0=ALU.is_ge),
                reads=[("lg",), ("mx",)], writes=[("eq",)])
            p.step("vector", lambda v: v.tensor_copy(
                msk_sb[:, 0:1], lg_sb[:, NE:NE + 1]),
                reads=[("eq",)], writes=[("msk",)])
            p.step("vector", lambda v: v.tensor_scalar(
                out=msk_sb[:, NE:NE + 1], in0=lg_sb[:, NE:NE + 1],
                scalar1=-1.0, scalar2=1.0, op0=ALU.mult, op1=ALU.add),
                reads=[("eq",)], writes=[("mskc",)])
            for e in range(1, NE):
                p.step("vector", lambda v, e=e: v.tensor_tensor(
                    out=msk_sb[:, e:e + 1], in0=lg_sb[:, NE + e:NE + e + 1],
                    in1=msk_sb[:, NE:NE + 1], op=ALU.mult),
                    reads=[("eq",), ("mskc",)], writes=[("msk",)])
                if e < NE - 1:
                    p.step("vector", lambda v, e=e: v.tensor_tensor(
                        out=msk_sb[:, NE:NE + 1], in0=msk_sb[:, NE:NE + 1],
                        in1=msk_sb[:, e:e + 1], op=ALU.subtract),
                        reads=[("msk",)], writes=[("mskc",)])
            for e in range(NE):
                ib, bank = pool.take()
                p.step("tensor", lambda t, e=e, b=bank: t.transpose(
                    b[0:1, 0:128], msk_sb[:, e:e + 1], ident[:]),
                    reads=[("msk",), ("ident",)], writes=[("pb", ib)])
                p.step("vector", lambda v, e=e, gi=gi, b=bank: v.tensor_copy(
                    kvz_sb[0:1, 4 + e, gi * GRP:(gi + 1) * GRP], b[0:1, 0:128]),
                    reads=[("pb", ib)], writes=[("mskrow", e)])
                pool.give(ib)

        for e in range(NE):
            ib, bank = pool.take()
            p.step("tensor", lambda t, e=e, b=bank: t.matmul(
                b[:], onesr[:], kvz_sb[0:1, 4 + e, :], start=True, stop=True),
                reads=[("mskrow", e), ("onesr",)], writes=[("pb", ib)])
            p.step("vector", lambda v, b=bank: v.tensor_copy(mbcE_sb, b[:]),
                   reads=[("pb", ib)], writes=[("mbcE",)])
            pool.give(ib)
            for par in range(2):
                p.dma("sync", f"wa{par}", wA_sb[:, par], wslab(we1d[e], par),
                      writes=[("wA", par)])
            for fc in range(FC):
                par = fc % 2
                ib, bank = pool.take()
                for kc in range(DC):
                    p.step("tensor", lambda t, par=par, kc=kc, b=bank,
                           st=(kc == 0), sp=(kc == DC - 1):
                           t.matmul(b[:], wA_sb[:, par, kc], t1_sb[:, kc],
                                    start=st, stop=sp),
                           reads=[("wA", par), ("t1",)], writes=[("pb", ib)])
                p.step("scalar", lambda s, e=e, fc=fc, b=bank: s.activation(
                    out=h1_sb[:, fc], in_=b[:], func=AF.Relu,
                    bias=be1_sb[:, e, fc:fc + 1]),
                    reads=[("pb", ib), ("be1",)], writes=[("h1",)])
                pool.give(ib)
                if fc + 2 < FC:
                    p.dma("sync", f"wa{par}", wA_sb[:, par],
                          wslab(we1d[e], fc + 2), writes=[("wA", par)])
            for oc in range(DC):
                ib, bank = w2_big(we2d[e], None, only_oc=oc)
                if e == 0:
                    p.step("vector", lambda v, e=e, oc=oc, b=bank:
                           v.scalar_tensor_tensor(
                        out=cat_sb[:, oc], in0=b[:],
                        scalar=be2_sb[:, e, oc:oc + 1], in1=mbcE_sb,
                        op0=ALU.add, op1=ALU.mult),
                        reads=[("pb", ib), ("be2",), ("mbcE",)], writes=[("cat",)])
                else:
                    p.step("vector", lambda v, e=e, oc=oc, b=bank:
                           v.scalar_tensor_tensor(
                        out=mscr_sb, in0=b[:],
                        scalar=be2_sb[:, e, oc:oc + 1], in1=mbcE_sb,
                        op0=ALU.add, op1=ALU.mult),
                        reads=[("pb", ib), ("be2",), ("mbcE",)], writes=[("mscr",)])
                    p.step("vector", lambda v, oc=oc: v.tensor_tensor(
                        out=cat_sb[:, oc], in0=cat_sb[:, oc], in1=mscr_sb,
                        op=ALU.add),
                        reads=[("mscr",)], writes=[("cat",)])
                pool.give(ib)

        for oc in range(DC):
            p.step("vector", lambda v, oc=oc: v.tensor_tensor(
                out=u_sb[:, oc], in0=u_sb[:, oc], in1=cat_sb[:, oc],
                op=ALU.add),
                reads=[("cat",)], writes=[("u",)])
        p.dma("sync", "od",
              yT.rearrange("(c p) s -> p c s", p=128)[:, :, t0:t0 + TILE],
              u_sb[:], reads=[("u",)])

    total_od = p.cnt["od"]
    p.step("vector", lambda e: e.engine_nop(), deps=[("od", total_od)])

    p.build()
    return nc


_NC_CACHE = None


def kernel(**inputs):
    global _NC_CACHE
    g = lambda n: np.asarray(inputs[n], dtype=np.float32)
    x = g("L0_attn_x")
    wqv = np.ascontiguousarray(g("L0_attn_wq").reshape(D, D))
    wkv = np.ascontiguousarray(g("L0_attn_wk").reshape(D, D))
    wvv = np.ascontiguousarray(g("L0_attn_wv").reshape(D, D))
    wov = np.ascontiguousarray(g("L0_attn_wo"))
    omT = np.ascontiguousarray(g("L0_attn_omega").T)
    ln1g = np.ascontiguousarray(g("L0_attn_ln1_g").reshape(DC, 128).T)
    ln1b = np.ascontiguousarray(g("L0_attn_ln1_b").reshape(DC, 128).T)
    ln2g = np.ascontiguousarray(g("L0_attn_ln2_g").reshape(DC, 128).T)
    ln2b = np.ascontiguousarray(g("L0_attn_ln2_b").reshape(DC, 128).T)
    w1 = np.ascontiguousarray(g("L0_attn_w1"))
    b1 = np.ascontiguousarray(g("L0_attn_b1").reshape(FC, 128).T)
    w2 = np.ascontiguousarray(g("L0_attn_w2"))
    b2 = np.ascontiguousarray(g("L0_attn_b2").reshape(DC, 128).T)
    wr = np.ascontiguousarray(g("L0_ffn_Wr"))
    br = np.ascontiguousarray(g("L0_ffn_br").reshape(NE, 1))
    we1 = np.ascontiguousarray(g("L0_ffn_We1"))
    be1 = np.ascontiguousarray(g("L0_ffn_be1").reshape(NE, FC, 128).transpose(2, 0, 1))
    we2 = np.ascontiguousarray(g("L0_ffn_We2"))
    be2 = np.ascontiguousarray(g("L0_ffn_be2").reshape(NE, DC, 128).transpose(2, 0, 1))

    if _NC_CACHE is None:
        _NC_CACHE = build_core_program()
    nc = _NC_CACHE

    shared = dict(wq=wqv, wk=wkv, wv=wvv, wo=wov, omegaT=omT,
                  ln1_g=ln1g, ln1_b=ln1b, ln2_g=ln2g, ln2_b=ln2b,
                  w1=w1, b1=b1, w2=w2, b2=b2, wr=wr, br=br,
                  we1=we1, be1=be1, we2=we2, be2=be2)
    in_maps = []
    xTs = [np.ascontiguousarray(x[b].T) for b in range(B)]
    for c in range(8):
        b, half = c // 2, c % 2
        m = dict(shared)
        m["xTo"] = np.ascontiguousarray(xTs[b][:, half * T_OWN:(half + 1) * T_OWN])
        in_maps.append(m)

    res = run_bass_kernel_spmd(nc, in_maps, core_ids=list(range(8)))
    y = np.empty((B, S, D), dtype=np.float32)
    for c in range(8):
        b, half = c // 2, c % 2
        y[b, half * T_OWN:(half + 1) * T_OWN, :] = np.asarray(
            res.results[c]["yT"], dtype=np.float32).T
    return y
